# version 09
# baseline (speedup 1.0000x reference)
import numpy as np
import ml_dtypes
from contextlib import ExitStack

import concourse.bass as bass
import concourse.mybir as mybir
from concourse.bass_utils import run_bass_kernel_spmd
from concourse.alu_op_type import AluOpType as OP

F32 = mybir.dt.float32
BF16 = mybir.dt.bfloat16
AF = mybir.ActivationFunctionType
AX = mybir.AxisListType

T = 2048
D = 1024
NB = 2
LH, DH, WIN = 8, 64, 512
HID = 2730
HIDP = 2816
NKT = HIDP // 128
KTB = (11, 11)
QK_SCALE = 8.0


def _host_tables():
    bf = ml_dtypes.bfloat16
    half = np.arange(0, DH, 2, dtype=np.float64)
    inv_freq = 1.0 / 10000.0 ** (half / DH)
    t = np.arange(T, dtype=np.float64)
    freqs = np.concatenate([t[:, None] * inv_freq[None, :]] * 2, axis=-1)
    base = (half + 0.4 * DH) / (1.4 * DH)
    power = (t - T // 2) / float(WIN // 2)
    scale = np.concatenate([base[None, :] ** power[:, None]] * 2, axis=-1)
    cos, sin = np.cos(freqs), np.sin(freqs)
    sign = np.where(np.arange(DH) < 32, -1.0, 1.0)
    swap = (np.arange(DH) + 32) % DH
    qc = (QK_SCALE * cos * scale).T
    qs = ((QK_SCALE * sin * scale).T * sign[:, None])[swap, :]
    kc = (cos / scale).T
    ks = ((sin / scale).T * sign[:, None])[swap, :]
    dup = lambda a: np.concatenate([a, a], axis=0).astype(bf)
    tabs = np.stack([dup(qc), dup(qs), dup(kc), dup(ks)])

    sel = np.zeros((128, 128), np.float32)
    for d in range(64):
        sel[swap[d], d] = 1.0
        sel[64 + swap[d], 64 + d] = 1.0

    p = np.arange(128)[:, None]
    j = np.arange(256)[None, :]
    one = np.ones((128, 256), np.float32)
    L2 = np.concatenate([(j <= p), (j <= p + 128),
                         (j >= p), (j >= p + 128)], axis=1).astype(np.float32)
    u = np.arange(896)[None, :]
    Dm = (u >= p + 384).astype(np.float32)

    sqsel = np.zeros((2, 128, 4), np.float32)
    for pj in range(2):
        sqsel[pj, 0:64, 2 * pj] = 1.0
        sqsel[pj, 64:128, 2 * pj + 1] = 1.0
    bcsel = np.zeros((2, 68, 128), np.float32)
    for pj in range(2):
        bcsel[pj, 2 * pj, 0:64] = 1.0
        bcsel[pj, 2 * pj + 1, 64:128] = 1.0
        bcsel[pj, 64 + 2 * pj, 0:64] = 1.0
        bcsel[pj, 64 + 2 * pj + 1, 64:128] = 1.0
    onesf = np.ones((1, 128), np.float32)
    return dict(c_tabs=np.ascontiguousarray(tabs.transpose(1, 0, 2)),
                c_sel=np.ascontiguousarray(sel.astype(bf)),
                c_ml=np.ascontiguousarray(L2.astype(bf)),
                c_md=np.ascontiguousarray(Dm.astype(bf)),
                c_sqsel=np.ascontiguousarray(sqsel.astype(bf).transpose(1, 0, 2)),
                c_bcsel=np.ascontiguousarray(bcsel.transpose(1, 0, 2)),
                c_onesf=onesf)


def _host_prep(inputs):
    f32 = np.float32
    w1 = np.asarray(inputs["L0_ffn_w1"], f32)
    w2 = np.asarray(inputs["L0_ffn_w2"], f32)
    b1 = np.asarray(inputs["L0_ffn_b1"], f32)
    w1p = np.zeros((2, D, 2 * HIDP), f32)
    w1p[:, :, 0:HID] = w1[:, :, 0:HID]
    w1p[:, :, HIDP:HIDP + HID] = w1[:, :, HID:2 * HID]
    w2p = np.zeros((2, HIDP, D), f32)
    w2p[:, 0:HID, :] = w2
    b1p = np.zeros((2, 2, 128, NKT), f32)
    for l in range(2):
        for h in range(2):
            v = np.zeros(HIDP, f32)
            v[0:HID] = b1[l, h * HID:(h + 1) * HID]
            b1p[l, h] = v.reshape(NKT, 128).T
    cols = lambda a: np.ascontiguousarray(
        np.asarray(a, f32).reshape(2, 8, 128).transpose(2, 0, 1))
    com = dict(
        wqkv_d=np.ascontiguousarray(np.asarray(inputs["L0_attn_w_qkv"], f32).T),
        wout_d=np.ascontiguousarray(np.asarray(inputs["L0_attn_w_out"], f32).T),
        f_wqkv=np.ascontiguousarray(np.asarray(inputs["L0_ffn_wqkv"], f32)),
        f_wo=np.ascontiguousarray(np.asarray(inputs["L0_ffn_wo"], f32)),
        f_w1=np.ascontiguousarray(w1p), f_w2=np.ascontiguousarray(w2p),
        p_b1=np.ascontiguousarray(b1p.transpose(2, 0, 1, 3)),
        p_lnb=cols(inputs["L0_ffn_ln_b"]), p_b2=cols(inputs["L0_ffn_b2"]),
        p_lng=cols(inputs["L0_ffn_ln_g"]),
        p_qs=np.ascontiguousarray(
            np.tile(np.asarray(inputs["L0_ffn_q_scale"], f32), (1, 2))
            .T[:, :, None]),
        p_ks=np.ascontiguousarray(
            np.tile(np.asarray(inputs["L0_ffn_k_scale"], f32), (1, 2))
            .T[:, :, None]),
    )
    com.update(_host_tables())
    x = np.asarray(inputs["L0_attn_x"], np.float32)
    maps = []
    for c in range(8):
        m = dict(com)
        m["xT"] = np.ascontiguousarray(x[2 * c:2 * c + 2].transpose(0, 2, 1))
        maps.append(m)
    return maps


ENGS = ("t", "v", "s", "g", "y")


class Plan:
    def __init__(self, nc):
        self.nc = nc
        self.ops = {e: [] for e in ENGS}
        self.sems = {}
        for s in ("W", "X", "Tm", "Vm", "Sm", "Gm"):
            self.sems[s] = [nc.alloc_semaphore(s), 0]

    def tok(self, name):
        return (name, self.sems[name][1])

    def op(self, eng, fn, waits=(), inc=None, inck=1):
        ws = [w for w in waits if w is not None and w[1] > 0]
        rec = [fn, ws, None]
        if inc is not None:
            e = self.sems[inc]
            e[1] += inck
            rec[2] = (e[0], inck)
        self.ops[eng].append(rec)
        return None if inc is None else (inc, self.sems[inc][1])

    def emit(self, block):
        names = {"t": "tensor", "v": "vector", "s": "scalar", "g": "gpsimd",
                 "y": "sync"}
        for e in ENGS:
            ops = self.ops[e]

            def run(engine, ops=ops):
                seen = {}
                for fn, waits, inc in ops:
                    for sn, v in waits:
                        if seen.get(sn, 0) < v:
                            engine.wait_ge(self.sems[sn][0], v)
                            seen[sn] = v
                    ins = fn(engine)
                    if inc is not None:
                        ins.then_inc(*inc)
            getattr(block, names[e])(run)


class Builder:
    def __init__(self):
        nc = bass.Bass("TRN2", target_bir_lowering=False)
        self.nc = nc
        self.ctx = ExitStack()

        di = lambda n, s, dt=F32: nc.dram_tensor(n, s, dt, kind="ExternalInput")
        self.xT = di("xT", [NB, D, T])
        self.wqkv_d = di("wqkv_d", [D, 3 * D])
        self.wout_d = di("wout_d", [D, D])
        self.f_wqkv = di("f_wqkv", [2, D, 1536])
        self.f_wo = di("f_wo", [2, 512, D])
        self.f_w1 = di("f_w1", [2, D, 2 * HIDP])
        self.f_w2 = di("f_w2", [2, HIDP, D])
        self.p_b1 = di("p_b1", [128, 2, 2, NKT])
        self.p_lnb = di("p_lnb", [128, 2, 8])
        self.p_b2 = di("p_b2", [128, 2, 8])
        self.p_lng = di("p_lng", [128, 2, 8])
        self.p_qs = di("p_qs", [128, 2, 1])
        self.p_ks = di("p_ks", [128, 2, 1])
        self.c_tabs = di("c_tabs", [128, 4, T], BF16)
        self.c_sel = di("c_sel", [128, 128], BF16)
        self.c_ml = di("c_ml", [128, 1024], BF16)
        self.c_md = di("c_md", [128, 896], BF16)
        self.c_sqsel = di("c_sqsel", [128, 2, 4], BF16)
        self.c_bcsel = di("c_bcsel", [68, 2, 128])
        self.c_onesf = di("c_onesf", [1, 128])
        self.outT = nc.dram_tensor("outT", [NB, D, T], F32, kind="ExternalOutput")
        self.spill = nc.dram_tensor("spill", [D, T], F32, kind="Internal")
        self.w1b = nc.dram_tensor("w1b", [2, D, 2 * HIDP], BF16, kind="Internal")
        self.w2b = nc.dram_tensor("w2b", [2, HIDP, D], BF16, kind="Internal")

        sb = lambda n, s, dt: self.ctx.enter_context(nc.sbuf_tensor(n, s, dt))
        self.XF = sb("XF", [128, 8, T], F32)
        self.HS = sb("HS", [128, 8, T], BF16)
        self.OVR = sb("OVR", [128, 20608], BF16)
        self.TMP = sb("TMP", [128, 512], F32)
        self.Q2B = sb("Q2B", [128, T], BF16)
        self.ATB = sb("ATB", [128, 6, 512], BF16)
        self.XBF = self.ATB
        self.TAB = sb("TAB", [128, 4, T], BF16)
        self.SEL = sb("SEL", [128, 128], BF16)
        self.MSL = sb("MSL", [128, 1024], BF16)
        self.MSD = sb("MSD", [128, 896], BF16)
        self.WPL = sb("WPL", [128, 4, 2048], BF16)
        self.SQS = sb("SQS", [128, 2, 4], BF16)
        self.BCS = sb("BCS", [68, 2, 128], F32)
        self.ONF = sb("ONF", [1, 128], F32)
        self.ONB = sb("ONB", [128, 1], BF16)
        self.ST16 = sb("ST16", [68, T], F32)
        self.B1C = sb("B1C", [128, 2, 2, NKT], F32)
        self.LNB = sb("LNB", [128, 2, 8], F32)
        self.B2C = sb("B2C", [128, 2, 8], F32)
        self.LNG = sb("LNG", [128, 2, 8], F32)
        self.QSC = sb("QSC", [128, 2, 1], F32)
        self.KSC = sb("KSC", [128, 2, 1], F32)
        self.MBLK = sb("MBLK", [128, 17], F32)
        self.MFIN = sb("MFIN", [128, 32], BF16)
        self.MRD = sb("MRD", [128, 2], F32)
        self.OSTG = sb("OSTG", [128, 2, 512], BF16)
        self.NC3 = sb("NC3", [128, 32], F32)
        self.CRW = sb("CRW", [1, 512], F32)
        self.RRW = sb("RRW", [1, 512], F32)
        self.RBC = sb("RBC", [64, 512], F32)
        self.SRW = sb("SRW", [1, 512], F32)
        self.QRW = sb("QRW", [1, 512], F32)
        self.EPS = sb("EPS", [128, 1], F32)
        self.EP2 = sb("EP2", [128, 1], F32)
        self.ZRC = sb("ZRC", [128, 1], F32)

        ov = nc.lookup_mloc(self.OVR).addr
        at = lambda n, s, dt, off: nc.alloc_sbuf_tensor_at(n, s, dt, offset=ov + off)
        self.QG = at("QG", [128, 2, T], BF16, 0)
        self.GLU = at("GLU", [128, 11, 1024], BF16, 0)
        self.KG = at("KG", [128, 2, T], BF16, 8192)
        self.VG = at("VG", [128, 16, 260], BF16, 16384)
        self.OLB = at("OLB", [128, 4, T], BF16, 24704)
        self.VDG = at("VDG", [128, 16, 260], BF16, 0)
        self.ODB = at("ODB", [128, 8, T], BF16, 8448)
        xf = nc.lookup_mloc(self.XF).addr
        self.QD = nc.alloc_sbuf_tensor_at("QD", [128, 8, T], BF16, offset=xf)
        self.KD = nc.alloc_sbuf_tensor_at("KD", [128, 8, T], BF16, offset=xf + 32768)

        ps = lambda n, s: self.ctx.enter_context(nc.psum_tensor(n, s, F32))
        self.SC = ps("SC", [128, 4, 512])
        self.PV0 = ps("PV0", [128, 512])
        self.PV1 = ps("PV1", [128, 512])
        self.GA = ps("GA", [128, 512])
        self.GB = ps("GB", [128, 512])

        self.pl = Plan(nc)
        self.free = {}
        self.prod_t = {}
        self.wslot_free = [None] * 4
        self.ucr = 0
        self.scr = 0
        self.wsl_rr = -1
        self.pend = []

    def produce(self, res, tok):
        self.prod_t[res] = tok

    def prod(self, res):
        return self.prod_t.get(res)

    def consume(self, res, tok):
        self.free[res] = tok

    def fre(self, res):
        return self.free.get(res)

    def dma_g(self, dst, src, waits=(), accum=None):
        if accum is None:
            fn = lambda g, a=dst, b=src: g.dma_start(out=a, in_=b)
        else:
            fn = lambda g, a=dst, b=src, ac=accum: g.dma_start(out=a, in_=b,
                                                               accum_op=ac)
        return self.pl.op("g", fn, waits=waits, inc="W", inck=16)

    def dma_y(self, dst, src, waits=()):
        return self.pl.op("y", lambda y, a=dst, b=src: y.dma_start(out=a, in_=b),
                          waits=waits, inc="X", inck=16)

    def mm(self, out, lhsT, rhs, start, stop, waits=(), inc=None, tp=None):
        return self.pl.op(
            "t", lambda t, o=out, l=lhsT, r=rhs, a=start, b=stop, tp=tp:
            t.matmul(o, l, r, start=a, stop=b, tile_position=tp),
            waits=waits, inc=inc)

    def wslot_pick(self):
        self.wsl_rr = (self.wsl_rr + 1) % 4
        return self.wsl_rr

    def ring6(self, i):
        i = i % 6
        if i < 4:
            return self.SC[:, i, :], ("SC", i), i
        return (self.GA[:, :], "GA", i) if i == 4 else (self.GB[:, :], "GB", i)

    def pump(self, n):
        while len(self.pend) > n:
            self.pend.pop(0)()

    def wp(self, slot, a):
        return self.WPL[:, slot, :].rearrange("p (a c) -> p a c", a=a)

    def preamble(self):
        pl, dy = self.pl, self.dma_y
        dy(self.TAB[...], self.c_tabs[...])
        dy(self.SEL[...], self.c_sel[...])
        dy(self.MSL[...], self.c_ml[...])
        dy(self.MSD[...], self.c_md[...])
        dy(self.SQS[...], self.c_sqsel[...])
        dy(self.BCS[...], self.c_bcsel[...])
        dy(self.ONF[...], self.c_onesf[...])
        dy(self.B1C[...], self.p_b1[...])
        dy(self.LNB[...], self.p_lnb[...])
        dy(self.B2C[...], self.p_b2[...])
        dy(self.LNG[...], self.p_lng[...])
        dy(self.QSC[...], self.p_qs[...])
        dy(self.KSC[...], self.p_ks[...])
        self.tk_const = pl.tok("X")
        pl.op("g", lambda g: g.memset(self.EPS[...], 1e-5))
        pl.op("g", lambda g: g.memset(self.EP2[...], 1e-24))
        pl.op("g", lambda g: g.memset(self.ZRC[...], 0.0))
        pl.op("g", lambda g: g.memset(self.ONB[...], 1.0 / 1024.0), inc="Gm")
        self.tk_ones = pl.tok("Gm")
        self.tk_x = dy(self.XF.ap(),
                       self.xT[0].rearrange("(d p) t -> p d t", p=128))
        pieces = []
        for l in range(2):
            for cc in range(0, 2 * HIDP, 256):
                pieces.append((self.f_w1[l].rearrange("(a p) c -> p a c", p=128)
                               [:, :, cc:cc + 256],
                               self.w1b[l].rearrange("(a p) c -> p a c", p=128)
                               [:, :, cc:cc + 256]))
            for rr in range(0, HIDP, 256):
                pieces.append((self.f_w2[l, rr:rr + 256]
                               .rearrange("(a p) c -> p a c", p=128),
                               self.w2b[l, rr:rr + 256]
                               .rearrange("(a p) c -> p a c", p=128)))
        self._precast_pieces = pieces
        self.tk_precast = None

    def emit_precast(self):
        for srcp, dstp in self._precast_pieces:
            a = srcp.shape[1]
            slot = self.wslot_pick()
            slotap = self.wp(slot, a)
            tk_ld = self.dma_g(slotap, srcp, waits=[self.wslot_free[slot]])
            self.wslot_free[slot] = self.dma_y(dstp, slotap, waits=[tk_ld])
        self._precast_pieces = []
        self.tk_precast = self.pl.tok("X")

    def build(self):
        self.preamble()
        for b in range(NB):
            tk_x = self.tk_x if b == 0 else self.tk_xnext
            self.produce("XF", tk_x)
            self.macaron_ffn(b)
            self.dense_attn(b)
            self.macaron_ffn(b)
            tk_st = self.dma_y(self.outT[b].rearrange("(d p) t -> p d t", p=128),
                               self.XF.ap(), waits=[self.prod("XF")])
            self.consume("XF", tk_st)
            if b + 1 < NB:
                self.tk_xnext = self.dma_y(
                    self.XF.ap(),
                    self.xT[b + 1].rearrange("(d p) t -> p d t", p=128),
                    waits=[tk_st])
        self.pl.op("y", lambda y: y.nop(), waits=[self.pl.tok("X")])
        with self.nc.Block() as block:
            self.pl.emit(block)
        return self.nc

    def macaron_ffn(self, b):
        pl = self.pl
        tk_x = self.prod("XF")
        tk_spill = self.dma_y(self.spill.ap().rearrange("(d p) t -> p d t", p=128),
                              self.XF.ap(), waits=[tk_x, self.fre("spill")])
        last = self.layer_norm(self.XF, "XF", None, tk_x, apply_extra=tk_spill)
        self.produce("XF", last)
        for l in range(2):
            self.inner_layer(b, l)
        tk_h = pl.op("v", lambda v: v.tensor_scalar_mul(
            self.XF.ap().rearrange("p d t -> p (d t)"),
            self.XF.ap().rearrange("p d t -> p (d t)"), 0.5),
            waits=[self.prod("XF")], inc="Vm")
        tk_add = self.dma_g(self.XF.ap(),
                            self.spill.ap().rearrange("(d p) t -> p d t", p=128),
                            waits=[tk_h, tk_spill], accum=OP.add)
        self.consume("spill", tk_add)
        self.produce("XF", tk_add)

    def layer_norm(self, src_f32, dst, affine_l, tk_src, apply_extra=None):
        pl, mm = self.pl, self.mm
        stats_from_hs = src_f32 is None
        last = None
        chain = [None] * 4

        def stats(c):
            cs0 = c * 512
            ma = self.SC[0:1, (c % 2) * 2, :]
            qa = self.SC[0:1, (c % 2) * 2 + 1, :]
            mk_, qk_ = ("SC", (c % 2) * 2), ("SC", (c % 2) * 2 + 1)
            for dt in range(8):
                if not stats_from_hs:
                    tc = pl.op("s", lambda s, dt=dt, cs0=cs0: s.copy(
                        self.XBF[:, dt % 2, :], src_f32[:, dt, cs0:cs0 + 512]),
                        waits=[tk_src, self.fre(("ATB", dt % 2))], inc="Sm")
                    xb = self.XBF[:, dt % 2, :]
                else:
                    tc = tk_src
                    xb = self.HS[:, dt, cs0:cs0 + 512]
                mm(ma, self.ONB[:, :], xb, dt == 0, dt == 7,
                   waits=[tc, self.tk_ones, self.fre(mk_) if dt == 0 else None])
                tq = pl.op("s", lambda s, xb=xb: s.activation(
                    self.Q2B[:, 0:512], xb, AF.Square),
                    waits=[tc, self.fre(("q2", 0))], inc="Sm")
                if not stats_from_hs:
                    self.consume(("ATB", dt % 2), tq)
                tsq = mm(qa, self.ONB[:, :], self.Q2B[:, 0:512],
                         dt == 0, dt == 7,
                         waits=[tq, self.fre(qk_) if dt == 0 else None],
                         inc="Tm")
                self.consume(("q2", 0), tsq)
            sr = self.SRW if c % 2 == 0 else self.RRW
            qr = self.QRW if c % 2 == 0 else self.CRW
            srk = "SRW" if c % 2 == 0 else ("RR", 0)
            qrk = "QRW" if c % 2 == 0 else "CRW"
            tk_mm = pl.tok("Tm")
            pl.op("v", lambda v, sr=sr, ma=ma: v.tensor_copy(sr[...], ma),
                  waits=[tk_mm, self.fre(srk)])
            tk_rows = pl.op("v", lambda v, qr=qr, qa=qa: v.tensor_copy(
                qr[...], qa), waits=[self.fre(qrk)], inc="Vm")
            self.consume(mk_, tk_rows)
            self.consume(qk_, tk_rows)
            chain[c] = (sr, qr, srk, qrk, tk_rows)

        def bcast_apply(c):
            nonlocal last
            cs0 = c * 512
            sr, qr, srk, qrk, tk_rows = chain[c]
            ga, gak = (self.GA, "GA")
            gb, gbk = (self.GB, "GB")
            tga = mm(ga[:, :], self.ONF[...], sr[...], True, True,
                     waits=[tk_rows, self.fre(gak)], inc="Tm")
            pl.op("v", lambda v, sr=sr: v.tensor_mul(sr[...], sr[...], sr[...]),
                  waits=[tga])
            tk_var = pl.op("v", lambda v, sr=sr, qr=qr: v.tensor_sub(
                qr[...], qr[...], sr[...]), inc="Vm")
            self.consume(srk, tk_var)
            pl.op("s", lambda s, qr=qr: s.activation(qr[...], qr[...],
                                                     AF.Ln, bias=self.EPS[0:1, :]),
                  waits=[tk_var])
            tk_rstd = pl.op("s", lambda s, qr=qr: s.activation(
                qr[...], qr[...], AF.Exp, scale=-0.5), inc="Sm")
            tgb = mm(gb[:, :], self.ONF[...], qr[...], True, True,
                     waits=[tk_rstd, self.fre(gbk)], inc="Tm")
            self.consume(qrk, tgb)
            for dt in range(8):
                srcap = (self.HS[:, dt, cs0:cs0 + 512] if stats_from_hs
                         else src_f32[:, dt, cs0:cs0 + 512])
                pl.op("v", lambda v, srcap=srcap, ga=ga: v.tensor_tensor(
                    self.TMP[:, :], srcap, ga[:, :], OP.subtract),
                    waits=[tga, tgb, apply_extra, self.fre("TMP")])
                if dst == "XF":
                    last = pl.op("v", lambda v, dt=dt, cs0=cs0, gb=gb:
                                 v.tensor_tensor(
                                     src_f32[:, dt, cs0:cs0 + 512],
                                     self.TMP[:, :], gb[:, :], OP.mult),
                                 inc="Vm")
                    self.consume("TMP", last)
                    if dt == 7:
                        self.consume(gak, last)
                        self.consume(gbk, last)
                else:
                    tv = pl.op("v", lambda v, gb=gb: v.tensor_tensor(
                        self.TMP[:, :], self.TMP[:, :], gb[:, :],
                        OP.mult), inc="Vm")
                    if dt == 7:
                        self.consume(gak, tv)
                        self.consume(gbk, tv)
                    if affine_l is None:
                        last = pl.op("s", lambda s, dt=dt, cs0=cs0:
                                     s.copy(self.HS[:, dt, cs0:cs0 + 512],
                                            self.TMP[:, :]),
                                     waits=[tv, self.fre("HS")], inc="Sm")
                    else:
                        gc = self.LNG[:, affine_l, dt:dt + 1]
                        bc = self.LNB[:, affine_l, dt:dt + 1]
                        last = pl.op("v", lambda v, dt=dt, cs0=cs0, gc=gc,
                                     bc=bc: v.tensor_scalar(
                                         self.HS[:, dt, cs0:cs0 + 512],
                                         self.TMP[:, :], gc, bc,
                                         OP.mult, OP.add),
                                     waits=[tv, self.fre("HS"), self.tk_const],
                                     inc="Vm")
                    self.consume("TMP", last)

        stats(0)
        for c in range(1, 4):
            stats(c)
            bcast_apply(c - 1)
        bcast_apply(3)
        return last

    def shift_cast(self, tk_src):
        pl = self.pl
        last = None
        for dt in range(8):
            if dt >= 4:
                pl.op("s", lambda s, dt=dt: s.copy(
                    self.HS[:, dt, 1:T], self.XF[:, dt, 0:T - 1]),
                    waits=[tk_src, self.fre("HS")])
                last = pl.op("s", lambda s, dt=dt: s.copy(
                    self.HS[:, dt, 0:1], self.ZRC[...]), inc="Sm")
            else:
                last = pl.op("s", lambda s, dt=dt: s.copy(
                    self.HS[:, dt, :], self.XF[:, dt, :]),
                    waits=[tk_src, self.fre("HS")], inc="Sm")
        return last

    def inner_layer(self, b, l):
        pl, mm = self.pl, self.mm
        tk_hs = self.shift_cast(self.prod("XF"))
        self.produce("HS", tk_hs)
        for g in range(2):
            self.local_group(b, l, g, tk_hs)
            if self._precast_pieces:
                self.emit_precast()
        self.consume("HS", pl.tok("Tm"))
        wsl = []
        for s in range(2):
            slot = self.wslot_pick()
            tk = self.dma_g(self.wp(slot, 2),
                            self.f_wo[l, s * 256:(s + 1) * 256]
                            .rearrange("(a p) c -> p a c", p=128),
                            waits=[self.wslot_free[slot]])
            wsl.append((slot, tk))
        for mt in range(8):
            for c in range(4):
                pb, key = ((self.GA, "GA") if (mt * 4 + c) % 2 == 0
                           else (self.GB, "GB"))
                first = True
                tm = None
                for s in range(2):
                    for kk in range(2):
                        lastmm = (s == 1 and kk == 1)
                        tm = mm(pb[:, :], self.wp(wsl[s][0], 2)
                                [:, kk, mt * 128:mt * 128 + 128],
                                self.OLB[:, 2 * s + kk, c * 512:(c + 1) * 512],
                                first, lastmm,
                                waits=[wsl[s][1], self.prod(("O", 2 * s + kk)),
                                       self.fre(key) if first else None],
                                inc="Tm" if lastmm else None)
                        first = False
                tv = pl.op("v", lambda v, mt=mt, c=c, pb=pb: v.tensor_tensor(
                    self.XF[:, mt, c * 512:(c + 1) * 512],
                    self.XF[:, mt, c * 512:(c + 1) * 512], pb[:, :], OP.add),
                    waits=[tm, self.prod("XF")], inc="Vm")
                self.consume(key, tv)
        self.produce("XF", pl.tok("Vm"))
        for s in range(2):
            self.wslot_free[wsl[s][0]] = pl.tok("Tm")
        for pr in range(4):
            self.consume(("O", pr), pl.tok("Tm"))
        self.free["OVR2"] = pl.tok("Tm")
        self.free["OVRg"] = pl.tok("Tm")
        tk_hs2 = self.shift_cast(self.prod("XF"))
        tk_ln = self.layer_norm(None, "HS", l, tk_hs2)
        self.produce("HS", tk_ln)
        self.mlp(b, l, tk_ln)
        self.consume("HS", pl.tok("Tm"))

    def v_gemm(self, wsrc_ap, vbuf, tk_hs, tk_vones):
        pl, mm = self.pl, self.mm
        slot = self.wslot_pick()
        tk_wv = self.dma_g(self.wp(slot, 8), wsrc_ap,
                           waits=[self.wslot_free[slot]])
        vap = vbuf.ap()
        for tt in range(16):
            pb, key = (self.GA, "GA") if tt % 2 == 0 else (self.GB, "GB")
            tm = None
            for dt in range(8):
                tm = mm(pb[:, 0:256], self.HS[:, dt, tt * 128:tt * 128 + 128],
                        self.wp(slot, 8)[:, dt, :], dt == 0, dt == 7,
                        waits=[tk_wv, tk_hs,
                               self.fre(key) if dt == 0 else None],
                        inc="Tm" if dt == 7 else None)
            tv = pl.op("v", lambda v, tt=tt, pb=pb, vap=vap: v.tensor_copy(
                bass.AP(vap.tensor, vap.offset + tt * 260,
                        [[vap.ap[0][0], 128], [65, 4], [1, 64]]),
                pb[:, 0:256]),
                waits=[tm, tk_vones, self.fre("VG"), self.fre("OVR")], inc="Vm")
            self.consume(key, tv)
        self.produce("VG", pl.tok("Vm"))
        self.wslot_free[slot] = pl.tok("Vm")

    def local_group(self, b, l, g, tk_hs):
        pl, mm = self.pl, self.mm
        vap = self.VG.ap()
        tk_vones = pl.op("g", lambda gg: gg.memset(
            bass.AP(vap.tensor, vap.offset + 64,
                    [[vap.ap[0][0], 128], [260, 16], [65, 4], [1, 1]]), 1.0),
            waits=[self.fre("VG"), self.fre("OVR")], inc="Gm")
        evk, tk_rk = self.qk_gemm(l, g, "k", tk_hs)
        evq, tk_rq = self.qk_gemm(l, g, "q", tk_hs)
        self.v_gemm(self.f_wqkv[l].rearrange("(a p) c -> p a c", p=128)
                    [:, :, 1024 + g * 256:1024 + g * 256 + 256],
                    self.VG, tk_hs, tk_vones)
        tkk = self.qk_rot(l, g, "k", evk, tk_rk)
        tkq = self.qk_rot(l, g, "q", evq, tk_rq)

        tmr = None
        for pj in range(2):
            for hh in range(2):
                base = 64 * hh
                for blk in range(8):
                    tmr = pl.op("v", lambda v, pj=pj, base=base, blk=blk:
                                v.tensor_reduce(
                                    self.MBLK[base:base + 64,
                                              pj * 8 + blk:pj * 8 + blk + 1],
                                    self.KG[base:base + 64, pj,
                                            blk * 256:(blk + 1) * 256],
                                    AX.X, OP.max, apply_absolute_value=True),
                                waits=[tkk, self.fre("MBLK")], inc="Vm")
        tk_mblk = tmr

        nc3 = {}
        for pj in range(2):
            tk_abs = pl.op("s", lambda s, pj=pj: s.activation(
                self.Q2B[:, :], self.QG[:, pj, :], AF.Abs),
                waits=[tkq] + [self.fre(("q2", c)) for c in range(4)], inc="Sm")
            for c in range(4):
                self.produce(("q2", c), tk_abs)
            nc3[pj] = self.pair_prep(True, g, pj, tk_abs, tk_mblk)
            for c in range(4):
                self.consume(("q2", c), pl.tok("Tm"))
        for pj in range(2):
            for hh in range(2):
                for qc in range(8):
                    self.attn_unit(True, g, pj, hh, qc, hh * 8 + qc,
                                   nc3[pj], 16 * pj)
        self.pump(0)
        for pj in range(2):
            self.consume(("NC3", 16 * pj), pl.tok("Sm"))
        self.consume("MBLK", pl.tok("Vm"))
        self.free[("QK", "q", g, 0)] = pl.tok("Tm")
        self.free[("QK", "q", g, 1)] = pl.tok("Tm")
        self.free[("QK", "k", g, 0)] = pl.tok("Tm")
        self.free[("QK", "k", g, 1)] = pl.tok("Tm")
        self.consume("VG", pl.tok("Tm"))

    def qk_gemm(self, l, g, side, tk_hs):
        pl, mm = self.pl, self.mm
        dstbuf = self.KG if side == "k" else self.QG
        col0 = (512 if side == "k" else 0) + g * 256
        bs = 0 if side == "k" else 64
        slot = self.wslot_pick()
        tk_w = self.dma_g(self.wp(slot, 8),
                          self.f_wqkv[l].rearrange("(a p) c -> p a c", p=128)
                          [:, :, col0:col0 + 256],
                          waits=[self.wslot_free[slot]])
        ev = {}
        for pj in range(2):
            for c in range(4):
                cs0 = c * 512
                pb, key = (self.GA, "GA") if c % 2 == 0 else (self.GB, "GB")
                tm = None
                for dt in range(8):
                    tm = mm(pb[:, :],
                            self.wp(slot, 8)[:, dt, pj * 128:pj * 128 + 128],
                            self.HS[:, dt, cs0:cs0 + 512], dt == 0, dt == 7,
                            waits=[tk_w, tk_hs,
                                   self.fre(key) if dt == 0 else None],
                            inc="Tm" if dt == 7 else None)
                te = pl.op("s", lambda s, pj=pj, cs0=cs0, pb=pb, dstbuf=dstbuf:
                           s.copy(dstbuf[:, pj, cs0:cs0 + 512], pb[:, :]),
                           waits=[tm, self.fre(("QK", side, g, pj)),
                                  self.fre("OVR")], inc="Sm")
                ev[(pj, c)] = te
                tq = pl.op("s", lambda s, cs0=cs0, pb=pb: s.activation(
                    self.Q2B[:, cs0:cs0 + 512], pb[:, :], AF.Square),
                    waits=[self.fre(("q2", c))], inc="Sm")
                self.consume(key, tq)
                tsq = mm(self.SC[0:4, c, :], self.SQS[:, pj, :],
                         self.Q2B[:, cs0:cs0 + 512], pj == 0, pj == 1,
                         waits=[tq, self.tk_const,
                                self.fre(("SC", c)) if pj == 0 else None],
                         inc="Tm")
                self.consume(("q2", c), tsq)
        tk_sq = pl.tok("Tm")
        tcp = pl.op("v", lambda v, bs=bs: v.tensor_copy(
            self.ST16[bs:bs + 4, :].rearrange("h (c t) -> h c t", c=4),
            self.SC[0:4, :, :]), waits=[tk_sq, self.fre(("ST", bs))],
            inc="Vm")
        for c in range(4):
            self.consume(("SC", c), tcp)
        pl.op("s", lambda s, bs=bs: s.activation(
            self.ST16[bs:bs + 4, :], self.ST16[bs:bs + 4, :], AF.Ln,
            bias=self.EP2[0:4, :]), waits=[tcp])
        tk_rstd = pl.op("s", lambda s, bs=bs: s.activation(
            self.ST16[bs:bs + 4, :], self.ST16[bs:bs + 4, :], AF.Exp,
            scale=-0.5), inc="Sm")
        self.wslot_free[slot] = pl.tok("Tm")
        return ev, tk_rstd

    def qk_rot(self, l, g, side, ev, tk_rstd):
        pl, mm = self.pl, self.mm
        dstbuf = self.KG if side == "k" else self.QG
        scol = self.KSC if side == "k" else self.QSC
        tabc = self.TAB[:, 2 if side == "k" else 0, :]
        tabs_ = self.TAB[:, 3 if side == "k" else 1, :]
        bs = 0 if side == "k" else 64
        last = None
        for pj in range(2):
            for c in range(4):
                cs0 = c * 512
                tba = mm(self.GA[:, :], self.BCS[bs:bs + 4, pj, :],
                         self.ST16[bs:bs + 4, cs0:cs0 + 512],
                         True, True,
                         waits=[tk_rstd, self.tk_const, self.fre("GA")],
                         inc="Tm")
                tsc = pl.op("v", lambda v, pj=pj, cs0=cs0, dstbuf=dstbuf,
                            scol=scol: v.scalar_tensor_tensor(
                                dstbuf[:, pj, cs0:cs0 + 512],
                                dstbuf[:, pj, cs0:cs0 + 512], scol[:, l, :],
                                self.GA[:, :], OP.mult, OP.mult),
                            waits=[tba, ev[(pj, c)], self.tk_const], inc="Vm")
                self.consume("GA", tsc)
                ta = pl.op("v", lambda v, pj=pj, cs0=cs0, dstbuf=dstbuf,
                           tabs_=tabs_: v.tensor_tensor(
                               self.Q2B[:, cs0:cs0 + 512],
                               dstbuf[:, pj, cs0:cs0 + 512],
                               tabs_[:, cs0:cs0 + 512], OP.mult),
                           waits=[self.fre(("q2", c))], inc="Vm")
                trm = mm(self.GB[:, :], self.SEL[...], self.Q2B[:, cs0:cs0 + 512],
                         True, True, waits=[ta, self.fre("GB")], inc="Tm")
                self.consume(("q2", c), trm)
                tb = pl.op("v", lambda v, pj=pj, cs0=cs0, dstbuf=dstbuf,
                           tabc=tabc: v.tensor_tensor(
                               self.TMP[:, :], dstbuf[:, pj, cs0:cs0 + 512],
                               tabc[:, cs0:cs0 + 512], OP.mult),
                           waits=[self.fre("TMP")])
                last = pl.op("v", lambda v, pj=pj, cs0=cs0, dstbuf=dstbuf:
                             v.tensor_tensor(dstbuf[:, pj, cs0:cs0 + 512],
                                             self.TMP[:, :], self.GB[:, :],
                                             OP.add), waits=[trm], inc="Vm")
                self.consume("GB", last)
                self.consume("TMP", last)
        self.consume(("ST", bs), pl.tok("Tm"))
        return last

    def pair_prep(self, local, g, pj, tk_abs, tk_mblk, qsrc=None):
        if qsrc is None:
            qsrc = self.Q2B
        """Batched negc3 bias columns for one pair's units into
        NC3[:, off + u]. off = 16*(pj%2). Returns token of the NC3 copies."""
        pl, mm = self.pl, self.mm
        nqc = 8 if local else 4
        qlen = 256 if local else 512
        off = 16 * (pj % 2)
        tred = {}
        tnc = None
        nu = 2 * nqc
        for hh in range(2):
            base = 64 * hh
            for qc in range(nqc):
                u = hh * nqc + qc
                q0 = qc * qlen
                if local:
                    blk0 = max(0, qc - 2)
                    tmf = pl.op("v", lambda v, base=base, pj=pj, blk0=blk0,
                                qc=qc, u=u: v.tensor_reduce(
                                    self.MRD[base:base + 64, u % 2:u % 2 + 1],
                                    self.MBLK[base:base + 64,
                                              pj * 8 + blk0:pj * 8 + qc + 1],
                                    AX.X, OP.max),
                                waits=[tk_mblk, self.fre(("MRD", u % 2))],
                                inc="Vm")
                    msrc = self.MRD[base:base + 64, u % 2:u % 2 + 1]
                else:
                    tmf = tk_mblk
                    msrc = self.MBLK[base:base + 64, pj:pj + 1]
                tmc = pl.op("s", lambda s, msrc=msrc, base=base, u=u, off=off:
                            s.activation(self.MFIN[base:base + 64,
                                                   off + u:off + u + 1],
                                         msrc, AF.Copy, scale=-1.0),
                            waits=[tmf, self.fre(("MFIN", off))], inc="Sm")
                if local:
                    self.consume(("MRD", u % 2), tmc)
                slot = u % 4
                tcm = mm(self.SC[0:1, slot, 0:qlen],
                         self.MFIN[base:base + 64, off + u:off + u + 1],
                         qsrc[base:base + 64, q0:q0 + qlen], True, True,
                         waits=[tmc, tk_abs, self.fre(("SC", slot))],
                         inc="Tm")
                trd = pl.op("v", lambda v, u=u, qlen=qlen, off=off, slot=slot:
                            v.tensor_reduce(
                                self.CRW[0:1, off + u:off + u + 1],
                                self.SC[0:1, slot, 0:qlen], AX.X, OP.min),
                            waits=[tcm,
                                   self.fre("CRW") if u == 0 else None],
                            inc="Vm")
                tred[u] = trd
                tbc = mm(self.SC[:, slot, 511:512], self.ONF[...],
                         self.CRW[0:1, off + u:off + u + 1], True, True,
                         waits=[trd], inc="Tm")
                self.consume(("SC", slot), tbc)
                if u % 4 == 3 or u == nu - 1:
                    r0 = (u // 4) * 4
                    nn = u - r0 + 1
                    tnc = pl.op("v", lambda v, r0=r0, nn=nn, off=off:
                                v.tensor_copy(
                                    self.NC3[:, off + r0:off + r0 + nn],
                                    self.SC[:, r0 % 4:r0 % 4 + nn, 511]
                                    if False else
                                    bass.AP(self.SC.ap().tensor,
                                            self.SC.ap().offset + (0) * 0 + 511,
                                            [[self.SC.ap().ap[0][0], 128],
                                             [512, nn]]),
                                    ),
                                waits=[tbc, self.fre(("NC3", off))
                                       if r0 == 0 else None], inc="Vm")
                    for uu in range(r0, u + 1):
                        self.consume(("SC", uu % 4), tnc)
        self.consume("CRW", pl.tok("Tm"))
        self.consume(("MFIN", off), pl.tok("Tm"))
        return tnc

    def attn_unit(self, local, g, pj, hh, qc, u, tk_nc3, off):
        pl, mm = self.pl, self.mm
        base = 64 * hh
        hg = 2 * pj + hh
        if local:
            qbuf, kbuf, vbuf, obuf = self.QG, self.KG, self.VG, self.OLB
            pslot, oslot = pj, 2 * g + pj
            q0, qlen = qc * 256, 256
            ms = [m_ for m_ in (-512, -384, -256, -128, 0, 128)
                  if qc * 256 + m_ >= 0]
            kts = [(qc * 256 + m_) // 128 for m_ in ms]
        else:
            qbuf, kbuf, vbuf, obuf = self.QD, self.KD, self.VDG, self.ODB
            pslot = oslot = 2 * g + pj
            q0, qlen = qc * 512, 512
            ms = None
            kts = list(range(4 * (qc + 1)))
        nkt = len(kts)
        ucr = self.ucr
        self.ucr += 1
        pvb = self.PV0 if ucr % 2 == 0 else self.PV1
        pvi = ucr % 2
        per = 512 // qlen
        vap = vbuf.ap()
        uctx = {"j": 0, "tk_pv": None}

        allsi = list(range(0, nkt, per))
        for blk0 in range(0, len(allsi), 3):
            self.pump(1)
            blksi = allsi[blk0:blk0 + 3]
            subs = []
            for si in blksi:
                sub = kts[si:si + per]
                ap6, key6, idx = self.ring6(self.scr)
                self.scr += 1
                tm = None
                for i, kt in enumerate(sub):
                    tm = mm(ap6[:, i * qlen:(i + 1) * qlen],
                            kbuf[base:base + 64, pslot, kt * 128:kt * 128 + 128],
                            qbuf[base:base + 64, pslot, q0:q0 + qlen],
                            True, True,
                            waits=[self.fre(key6) if i == 0 else None],
                            inc="Tm" if i == len(sub) - 1 else None)
                texp = pl.op("s", lambda s, ap6=ap6, idx=idx, u=u, off=off:
                             s.activation(self.ATB[:, idx, :], ap6, AF.Exp,
                                          bias=self.NC3[:, off + u:off + u + 1]),
                             waits=[tm, tk_nc3, self.fre(("ATB", idx))],
                             inc="Sm")
                self.consume(key6, texp)
                tmk = texp
                if local:
                    msl = None
                    if ms[si] == -512:
                        msl = self.MSL[:, 0:512]
                    elif ms[si] == 0:
                        msl = self.MSL[:, 512:1024]
                    if msl is not None:
                        tmk = pl.op("v", lambda v, idx=idx, msl=msl:
                                    v.tensor_tensor(self.ATB[:, idx, :],
                                                    self.ATB[:, idx, :], msl,
                                                    OP.mult), waits=[texp],
                                    inc="Vm")
                else:
                    sd = sub[0] - 4 * qc
                    if sd >= 0:
                        o2 = 384 - 128 * sd
                        tmk = pl.op("v", lambda v, idx=idx, o2=o2:
                                    v.tensor_tensor(self.ATB[:, idx, :],
                                                    self.ATB[:, idx, :],
                                                    self.MSD[:, o2:o2 + 512],
                                                    OP.mult), waits=[texp],
                                    inc="Vm")
                subs.append((sub, idx, tmk))

            def pvblk(subs=subs, uctx=uctx, pvb=pvb, pvi=pvi, qlen=qlen,
                      nkt=nkt, hg=hg, vap=vap):
                for sub, idx, tmk in subs:
                    tpv = None
                    for i, kt in enumerate(sub):
                        j = uctx["j"]
                        uctx["j"] += 1
                        lhsT = bass.AP(vap.tensor,
                                       vap.offset + kt * 260 + hg * 65,
                                       [[vap.ap[0][0], 128], [1, 65]])
                        tpv = mm(pvb[0:65, 0:qlen], lhsT,
                                 self.ATB[:, idx, i * qlen:(i + 1) * qlen],
                                 j == 0, j == nkt - 1,
                                 waits=[tmk, self.prod("VG"),
                                        self.fre(("PV", pvi)) if j == 0 else None],
                                 inc="Tm")
                    self.consume(("ATB", idx), tpv)
                uctx["tk_pv"] = pl.tok("Tm")
            self.pend.append(pvblk)

        def norm(uctx=uctx, pvb=pvb, pvi=pvi, qlen=qlen, base=base,
                 oslot=oslot, q0=q0, obuf=obuf, ucr=ucr):
            tk_pv = uctx["tk_pv"]
            rr = self.RRW if ucr % 2 == 0 else self.SRW
            rrk = ("RR", 0) if ucr % 2 == 0 else "SRW"
            tln = pl.op("s", lambda s, pvb=pvb, qlen=qlen, rr=rr: s.activation(
                rr[0:1, 0:qlen], pvb[64:65, 0:qlen], AF.Ln),
                waits=[tk_pv, self.fre(rrk)], inc="Sm")
            tcp = pl.op("v", lambda v, pvb=pvb, qlen=qlen, ucr=ucr:
                        v.tensor_copy(self.OSTG[0:64, ucr % 2, 0:qlen],
                                      pvb[0:64, 0:qlen]),
                        waits=[tk_pv, self.fre(("OST", ucr % 2))], inc="Vm")
            tex = pl.op("s", lambda s, qlen=qlen, rr=rr: s.activation(
                rr[0:1, 0:qlen], rr[0:1, 0:qlen], AF.Exp, scale=-1.0),
                inc="Sm")
            tb2 = mm(pvb[64:128, 0:qlen], self.ONF[0:1, 0:64],
                     rr[0:1, 0:qlen], True, True,
                     waits=[tex, tcp], inc="Tm", tp=(0, 64))
            self.free[rrk] = tb2
            tst = pl.op("v", lambda v, base=base, oslot=oslot, q0=q0,
                        qlen=qlen, obuf=obuf, ucr=ucr, pvb=pvb:
                        v.scalar_tensor_tensor(
                            obuf[base:base + 64, oslot, q0:q0 + qlen],
                            self.OSTG[0:64, ucr % 2, 0:qlen], 1.0,
                            pvb[64:128, 0:qlen], OP.mult, OP.mult),
                        waits=[tb2, self.fre("OVR2")], inc="Vm")
            self.consume(("PV", pvi), tst)
            self.consume(("OST", ucr % 2), tst)
            self.produce(("O", oslot), tst)
        self.pend.append(norm)

    def mlp(self, b, l, tk_ln):
        pl, mm = self.pl, self.mm
        ps8 = [lambda s=s: self.SC[:, s, :] for s in range(4)] + \
              [lambda: self.PV0[:, :], lambda: self.PV1[:, :],
               lambda: self.GA[:, :], lambda: self.GB[:, :]]
        ps8k = [("SC", 0), ("SC", 1), ("SC", 2), ("SC", 3),
                ("PV", 0), ("PV", 1), "GA", "GB"]
        for cb in range(2):
            tb0 = cb * 1024
            for ktbi, nkt_b in enumerate(KTB):
                kt0 = 11 * ktbi
                for hf in range(2):
                    hoff = hf * HIDP
                    for cc in range(kt0 * 128, (kt0 + nkt_b) * 128, 256):
                        ckts = min(2, kt0 + nkt_b - cc // 128)
                        slot = self.wslot_pick()
                        tkw = self.dma_g(
                            self.wp(slot, 8)[:, :, 0:128 * ckts],
                            self.w1b[l].rearrange("(a p) c -> p a c", p=128)
                            [:, :, hoff + cc:hoff + cc + 128 * ckts],
                            waits=[self.wslot_free[slot], self.tk_precast])
                        for ii in range(ckts):
                            kt = cc // 128 + ii
                            ks = kt - kt0
                            for n in range(2):
                                ns0 = tb0 + n * 512
                                pb, key = [[(self.GA, "GA"), (self.PV0, ("PV", 0))],
                                           [(self.GB, "GB"), (self.PV1, ("PV", 1))]][hf][n]
                                tm = None
                                for dt in range(8):
                                    tm = mm(pb[:, :],
                                            self.wp(slot, 8)[:, dt,
                                                             ii * 128:ii * 128 + 128],
                                            self.HS[:, dt, ns0:ns0 + 512],
                                            dt == 0, dt == 7,
                                            waits=[tkw, tk_ln,
                                                   self.fre(key) if dt == 0 else None],
                                            inc="Tm" if dt == 7 else None)
                                if hf == 0:
                                    te = pl.op(
                                        "v", lambda v, pb=pb, kt=kt, ks=ks, n=n:
                                        v.tensor_scalar_add(
                                            self.GLU[:, ks, n * 512:n * 512 + 512],
                                            pb[:, :],
                                            self.B1C[:, l, 0, kt:kt + 1]),
                                        waits=[tm, self.tk_const,
                                               self.fre(("GLU", ks, n)),
                                               self.fre("OVRg")],
                                        inc="Vm")
                                    self.consume(key, te)
                                    self.free[("GLUa", ks, n)] = te
                                else:
                                    tg = pl.op(
                                        "s", lambda s, pb=pb, kt=kt:
                                        s.activation(self.XBF[:, 0, :], pb[:, :],
                                                     AF.Gelu,
                                                     bias=self.B1C[:, l, 1, kt:kt + 1]),
                                        waits=[tm, self.tk_const,
                                               self.fre(("ATB", 0))], inc="Sm")
                                    self.consume(key, tg)
                                    tglu = pl.op(
                                        "v", lambda v, ks=ks, n=n:
                                        v.tensor_tensor(
                                            self.GLU[:, ks, n * 512:n * 512 + 512],
                                            self.GLU[:, ks, n * 512:n * 512 + 512],
                                            self.XBF[:, 0, :], OP.mult),
                                        waits=[tg, self.free.get(("GLUa", ks, n))],
                                        inc="Vm")
                                    self.consume(("ATB", 0), tglu)
                                    self.produce(("GLUr", ks, n), tglu)
                        self.wslot_free[slot] = pl.tok("Tm")
                for n in range(2):
                    ns0 = tb0 + n * 512
                    slot = None
                    for ki, kt in enumerate(range(kt0, kt0 + nkt_b)):
                        ks = kt - kt0
                        if ki % 2 == 0:
                            if slot is not None:
                                self.wslot_free[slot] = pl.tok("Tm")
                            nr = min(2, nkt_b - ki)
                            slot = self.wslot_pick()
                            tkw2 = self.dma_g(
                                self.wp(slot, 2)[:, 0:nr, :],
                                self.w2b[l, kt * 128:kt * 128 + 128 * nr]
                                .rearrange("(a p) c -> p a c", p=128),
                                waits=[self.wslot_free[slot], self.tk_precast])
                        tm = None
                        for mt in range(8):
                            tm = mm(ps8[mt](),
                                    self.wp(slot, 2)[:, ki % 2,
                                                     mt * 128:mt * 128 + 128],
                                    self.GLU[:, ks, n * 512:n * 512 + 512],
                                    ki == 0, ki == nkt_b - 1,
                                    waits=[tkw2, self.prod(("GLUr", ks, n)),
                                           self.fre(ps8k[mt]) if ki == 0 else None],
                                    inc="Tm" if mt == 7 else None)
                    self.wslot_free[slot] = pl.tok("Tm")
                    tmm = pl.tok("Tm")
                    for mt in range(8):
                        if ktbi == 0:
                            tv = pl.op("v", lambda v, mt=mt, ns0=ns0:
                                       v.tensor_tensor(
                                           self.XF[:, mt, ns0:ns0 + 512],
                                           self.XF[:, mt, ns0:ns0 + 512],
                                           ps8[mt](), OP.add),
                                       waits=[tmm, self.prod("XF")], inc="Vm")
                        else:
                            tv = pl.op("v", lambda v, mt=mt, ns0=ns0:
                                       v.scalar_tensor_tensor(
                                           self.XF[:, mt, ns0:ns0 + 512],
                                           ps8[mt](), self.B2C[:, l, mt:mt + 1],
                                           self.XF[:, mt, ns0:ns0 + 512],
                                           OP.add, OP.add),
                                       waits=[tmm, self.prod("XF"),
                                              self.tk_const], inc="Vm")
                        self.consume(ps8k[mt], tv)
                    for kk in range(nkt_b):
                        self.consume(("GLU", kk, n), pl.tok("Tm"))
                self.produce("XF", pl.tok("Vm"))
        self.produce("XF", pl.tok("Vm"))
        self.free["OVR"] = pl.tok("Tm")

    def dense_attn(self, b):
        pl, mm = self.pl, self.mm
        tk_ln = self.layer_norm(self.XF, "HS", None, self.prod("XF"))
        self.produce("HS", tk_ln)
        tk_spill = self.dma_y(self.spill.ap().rearrange("(d p) t -> p d t", p=128),
                              self.XF.ap(),
                              waits=[self.prod("XF"), self.fre("spill")])
        self.free["OVR"] = pl.tok("Tm")
        self.free["OVR2"] = pl.tok("Tm")
        for side in range(2):
            dst = self.QD if side == 0 else self.KD
            scl = 0.125 if side == 0 else 1.0
            for ch in range(4):
                col0 = side * 1024 + ch * 256
                slot = self.wslot_pick()
                tkw = self.dma_g(self.wp(slot, 8),
                                 self.wqkv_d.rearrange("(a p) c -> p a c", p=128)
                                 [:, :, col0:col0 + 256],
                                 waits=[self.wslot_free[slot]])
                for sub in range(2):
                    p = ch * 2 + sub
                    for c in range(4):
                        cs0 = c * 512
                        pb, key = ((self.GA, "GA") if (p * 4 + c) % 2 == 0
                                   else (self.GB, "GB"))
                        tm = None
                        for dt in range(8):
                            tm = mm(pb[:, :],
                                    self.wp(slot, 8)[:, dt, sub * 128:sub * 128 + 128],
                                    self.HS[:, dt, cs0:cs0 + 512], dt == 0,
                                    dt == 7,
                                    waits=[tkw, tk_ln, tk_spill,
                                           self.fre(key) if dt == 0 else None],
                                    inc="Tm" if dt == 7 else None)
                        te = pl.op("s", lambda s, p=p, cs0=cs0, pb=pb, dst=dst,
                                   scl=scl: s.activation(
                                       dst[:, p, cs0:cs0 + 512], pb[:, :],
                                       AF.Copy, scale=scl),
                                   waits=[tm], inc="Sm")
                        self.consume(key, te)
                self.wslot_free[slot] = pl.tok("Sm")
        tk_qk = pl.tok("Sm")
        self.produce("QD", tk_qk)
        self.produce("KD", tk_qk)

        for g in range(4):
            vap = self.VDG.ap()
            tk_vones = pl.op("g", lambda gg: gg.memset(
                bass.AP(vap.tensor, vap.offset + 64,
                        [[vap.ap[0][0], 128], [260, 16], [65, 4], [1, 1]]), 1.0),
                waits=[self.fre("VG"), self.fre("OVR")], inc="Gm")
            self.v_gemm(self.wqkv_d.rearrange("(a p) c -> p a c", p=128)
                        [:, :, 2048 + g * 256:2048 + g * 256 + 256],
                        self.VDG, tk_ln, tk_vones)

            nc3 = {}
            for pj in range(2):
                p = 2 * g + pj
                tmr = None
                for hh in range(2):
                    base = 64 * hh
                    tmr = pl.op("v", lambda v, base=base, p=p, pj=pj:
                                v.tensor_reduce(
                                    self.MBLK[base:base + 64, pj:pj + 1],
                                    self.KD[base:base + 64, p, :],
                                    AX.X, OP.max,
                                    apply_absolute_value=True),
                                waits=[tk_qk, self.fre("MBd")],
                                inc="Vm")
                tk_abs = pl.op("s", lambda s, p=p: s.activation(
                    self.Q2B[:, :], self.QD[:, p, :], AF.Abs),
                    waits=[tk_qk] + [self.fre(("q2", c)) for c in range(4)],
                    inc="Sm")
                for c in range(4):
                    self.produce(("q2", c), tk_abs)
                nc3[pj] = self.pair_prep(False, g, pj, tk_abs, tmr)
                self.free["MBd"] = pl.tok("Sm")
                for c in range(4):
                    self.consume(("q2", c), pl.tok("Tm"))
            for pj in range(2):
                for hh in range(2):
                    for qc in range(4):
                        self.attn_unit(False, g, pj, hh, qc, hh * 4 + qc,
                                       nc3[pj], 16 * pj)
            self.pump(0)
            for pj in range(2):
                self.consume(("NC3", 16 * pj), pl.tok("Sm"))

        tk_rel = self.dma_y(self.XF.ap(),
                            self.spill.ap().rearrange("(d p) t -> p d t", p=128),
                            waits=[tk_spill, pl.tok("Tm"), pl.tok("Sm")])
        self.consume("spill", tk_rel)
        for ch in range(4):
            slot = self.wslot_pick()
            tkw = self.dma_g(self.wp(slot, 8),
                             self.wout_d.rearrange("(a p) c -> p a c", p=128)
                             [:, :, ch * 256:ch * 256 + 256],
                             waits=[self.wslot_free[slot]])
            for sub in range(2):
                mt = ch * 2 + sub
                for c in range(4):
                    cs0 = c * 512
                    pb, key = ((self.GA, "GA") if (mt * 4 + c) % 2 == 0
                               else (self.GB, "GB"))
                    tm = None
                    for p in range(8):
                        tm = mm(pb[:, :],
                                self.wp(slot, 8)[:, p, sub * 128:sub * 128 + 128],
                                self.ODB[:, p, cs0:cs0 + 512], p == 0, p == 7,
                                waits=[tkw, self.prod(("O", p)),
                                       self.fre(key) if p == 0 else None],
                                inc="Tm" if p == 7 else None)
                    tv = pl.op("v", lambda v, mt=mt, cs0=cs0, pb=pb:
                               v.tensor_tensor(
                                   self.XF[:, mt, cs0:cs0 + 512],
                                   self.XF[:, mt, cs0:cs0 + 512], pb[:, :],
                                   OP.add), waits=[tm, tk_rel], inc="Vm")
                    self.consume(key, tv)
            self.wslot_free[slot] = pl.tok("Vm")
        self.consume("VG", pl.tok("Tm"))
        self.consume("HS", pl.tok("Tm"))
        self.free["OVR"] = pl.tok("Tm")
        self.free["OVR2"] = pl.tok("Tm")
        self.free["OVRg"] = pl.tok("Tm")
        for pr in range(4):
            self.free[("QK", "q", pr // 2, pr % 2)] = pl.tok("Tm")
            self.free[("QK", "k", pr // 2, pr % 2)] = pl.tok("Tm")
        self.produce("XF", pl.tok("Vm"))


def kernel(**inputs):
    maps = _host_prep(inputs)
    bld = Builder()
    nc = bld.build()
    res = run_bass_kernel_spmd(nc, maps, core_ids=list(range(8)))
    out = np.stack([r["outT"] for r in res.results])
    out = out.transpose(0, 1, 3, 2).reshape(16, T, D)
    return np.ascontiguousarray(out.astype(np.float32))


# version 14
# speedup vs baseline: 1.0183x; 1.0183x over previous
import numpy as np
import ml_dtypes
from contextlib import ExitStack

import concourse.bass as bass
import concourse.mybir as mybir
from concourse.bass_utils import run_bass_kernel_spmd
from concourse.alu_op_type import AluOpType as OP

F32 = mybir.dt.float32
BF16 = mybir.dt.bfloat16
AF = mybir.ActivationFunctionType
AX = mybir.AxisListType

T = 2048
D = 1024
NB = 2
LH, DH, WIN = 8, 64, 512
HID = 2730
HIDP = 2816
NKT = HIDP // 128
KTB = (11, 11)
QK_SCALE = 8.0


def _host_tables():
    bf = ml_dtypes.bfloat16
    half = np.arange(0, DH, 2, dtype=np.float64)
    inv_freq = 1.0 / 10000.0 ** (half / DH)
    t = np.arange(T, dtype=np.float64)
    freqs = np.concatenate([t[:, None] * inv_freq[None, :]] * 2, axis=-1)
    base = (half + 0.4 * DH) / (1.4 * DH)
    power = (t - T // 2) / float(WIN // 2)
    scale = np.concatenate([base[None, :] ** power[:, None]] * 2, axis=-1)
    cos, sin = np.cos(freqs), np.sin(freqs)
    sign = np.where(np.arange(DH) < 32, -1.0, 1.0)
    swap = (np.arange(DH) + 32) % DH
    qc = (QK_SCALE * cos * scale).T
    qs = ((QK_SCALE * sin * scale).T * sign[:, None])[swap, :]
    kc = (cos / scale).T
    ks = ((sin / scale).T * sign[:, None])[swap, :]
    dup = lambda a: np.concatenate([a, a], axis=0).astype(bf)
    tabs = np.stack([dup(qc), dup(qs), dup(kc), dup(ks)])

    sel = np.zeros((128, 128), np.float32)
    for d in range(64):
        sel[swap[d], d] = 1.0
        sel[64 + swap[d], 64 + d] = 1.0

    p = np.arange(128)[:, None]
    j = np.arange(256)[None, :]
    one = np.ones((128, 256), np.float32)
    L2 = np.concatenate([(j <= p), (j <= p + 128),
                         (j >= p), (j >= p + 128)], axis=1).astype(np.float32)
    u = np.arange(896)[None, :]
    Dm = (u >= p + 384).astype(np.float32)

    sqsel = np.zeros((2, 128, 4), np.float32)
    for pj in range(2):
        sqsel[pj, 0:64, 2 * pj] = 1.0
        sqsel[pj, 64:128, 2 * pj + 1] = 1.0
    bcsel = np.zeros((2, 68, 128), np.float32)
    for pj in range(2):
        bcsel[pj, 2 * pj, 0:64] = 1.0
        bcsel[pj, 2 * pj + 1, 64:128] = 1.0
        bcsel[pj, 64 + 2 * pj, 0:64] = 1.0
        bcsel[pj, 64 + 2 * pj + 1, 64:128] = 1.0
    onesf = np.ones((1, 128), np.float32)
    return dict(c_tabs=np.ascontiguousarray(tabs.transpose(1, 0, 2)),
                c_sel=np.ascontiguousarray(sel.astype(bf)),
                c_ml=np.ascontiguousarray(L2.astype(bf)),
                c_md=np.ascontiguousarray(Dm.astype(bf)),
                c_sqsel=np.ascontiguousarray(sqsel.astype(bf).transpose(1, 0, 2)),
                c_bcsel=np.ascontiguousarray(bcsel.transpose(1, 0, 2)),
                c_onesf=onesf)


def _host_prep(inputs):
    f32 = np.float32
    w1 = np.asarray(inputs["L0_ffn_w1"], f32)
    w2 = np.asarray(inputs["L0_ffn_w2"], f32)
    b1 = np.asarray(inputs["L0_ffn_b1"], f32)
    w1p = np.zeros((2, D, 2 * HIDP), f32)
    w1p[:, :, 0:HID] = w1[:, :, 0:HID]
    w1p[:, :, HIDP:HIDP + HID] = w1[:, :, HID:2 * HID]
    w2p = np.zeros((2, HIDP, D), f32)
    w2p[:, 0:HID, :] = w2
    b1p = np.zeros((2, 2, 128, NKT), f32)
    for l in range(2):
        for h in range(2):
            v = np.zeros(HIDP, f32)
            v[0:HID] = b1[l, h * HID:(h + 1) * HID]
            b1p[l, h] = v.reshape(NKT, 128).T
    cols = lambda a: np.ascontiguousarray(
        np.asarray(a, f32).reshape(2, 8, 128).transpose(2, 0, 1))
    com = dict(
        wqkv_d=np.ascontiguousarray(np.asarray(inputs["L0_attn_w_qkv"], f32).T),
        wout_d=np.ascontiguousarray(np.asarray(inputs["L0_attn_w_out"], f32).T),
        f_wqkv=np.ascontiguousarray(np.asarray(inputs["L0_ffn_wqkv"], f32)),
        f_wo=np.ascontiguousarray(np.asarray(inputs["L0_ffn_wo"], f32)),
        f_w1=np.ascontiguousarray(w1p), f_w2=np.ascontiguousarray(w2p),
        p_b1=np.ascontiguousarray(b1p.transpose(2, 0, 1, 3)),
        p_lnb=cols(inputs["L0_ffn_ln_b"]), p_b2=cols(inputs["L0_ffn_b2"]),
        p_lng=cols(inputs["L0_ffn_ln_g"]),
        p_qs=np.ascontiguousarray(
            np.tile(np.asarray(inputs["L0_ffn_q_scale"], f32), (1, 2))
            .T[:, :, None]),
        p_ks=np.ascontiguousarray(
            np.tile(np.asarray(inputs["L0_ffn_k_scale"], f32), (1, 2))
            .T[:, :, None]),
    )
    com.update(_host_tables())
    x = np.asarray(inputs["L0_attn_x"], np.float32)
    maps = []
    for c in range(8):
        m = dict(com)
        m["xT"] = np.ascontiguousarray(x[2 * c:2 * c + 2].transpose(0, 2, 1))
        maps.append(m)
    return maps


ENGS = ("t", "v", "s", "g", "y")


class Plan:
    def __init__(self, nc):
        self.nc = nc
        self.ops = {e: [] for e in ENGS}
        self.sems = {}
        for s in ("W", "X", "Tm", "Vm", "Sm", "Gm"):
            self.sems[s] = [nc.alloc_semaphore(s), 0]

    def tok(self, name):
        return (name, self.sems[name][1])

    def op(self, eng, fn, waits=(), inc=None, inck=1):
        ws = [w for w in waits if w is not None and w[1] > 0]
        rec = [fn, ws, None]
        if inc is not None:
            e = self.sems[inc]
            e[1] += inck
            rec[2] = (e[0], inck)
        self.ops[eng].append(rec)
        return None if inc is None else (inc, self.sems[inc][1])

    def emit(self, block):
        names = {"t": "tensor", "v": "vector", "s": "scalar", "g": "gpsimd",
                 "y": "sync"}
        for e in ENGS:
            ops = self.ops[e]

            def run(engine, ops=ops):
                seen = {}
                for fn, waits, inc in ops:
                    for sn, v in waits:
                        if seen.get(sn, 0) < v:
                            engine.wait_ge(self.sems[sn][0], v)
                            seen[sn] = v
                    ins = fn(engine)
                    if inc is not None:
                        ins.then_inc(*inc)
            getattr(block, names[e])(run)


class Builder:
    def __init__(self):
        nc = bass.Bass("TRN2", target_bir_lowering=False)
        self.nc = nc
        self.ctx = ExitStack()

        di = lambda n, s, dt=F32: nc.dram_tensor(n, s, dt, kind="ExternalInput")
        self.xT = di("xT", [NB, D, T])
        self.wqkv_d = di("wqkv_d", [D, 3 * D])
        self.wout_d = di("wout_d", [D, D])
        self.f_wqkv = di("f_wqkv", [2, D, 1536])
        self.f_wo = di("f_wo", [2, 512, D])
        self.f_w1 = di("f_w1", [2, D, 2 * HIDP])
        self.f_w2 = di("f_w2", [2, HIDP, D])
        self.p_b1 = di("p_b1", [128, 2, 2, NKT])
        self.p_lnb = di("p_lnb", [128, 2, 8])
        self.p_b2 = di("p_b2", [128, 2, 8])
        self.p_lng = di("p_lng", [128, 2, 8])
        self.p_qs = di("p_qs", [128, 2, 1])
        self.p_ks = di("p_ks", [128, 2, 1])
        self.c_tabs = di("c_tabs", [128, 4, T], BF16)
        self.c_sel = di("c_sel", [128, 128], BF16)
        self.c_ml = di("c_ml", [128, 1024], BF16)
        self.c_md = di("c_md", [128, 896], BF16)
        self.c_sqsel = di("c_sqsel", [128, 2, 4], BF16)
        self.c_bcsel = di("c_bcsel", [68, 2, 128])
        self.c_onesf = di("c_onesf", [1, 128])
        self.outT = nc.dram_tensor("outT", [NB, D, T], F32, kind="ExternalOutput")
        self.spill = nc.dram_tensor("spill", [D, T], F32, kind="Internal")
        self.w1b = nc.dram_tensor("w1b", [2, D, 2 * HIDP], BF16, kind="Internal")
        self.w2b = nc.dram_tensor("w2b", [2, HIDP, D], BF16, kind="Internal")

        sb = lambda n, s, dt: self.ctx.enter_context(nc.sbuf_tensor(n, s, dt))
        self.XF = sb("XF", [128, 8, T], F32)
        self.HS = sb("HS", [128, 8, T], BF16)
        self.OVR = sb("OVR", [128, 20608], BF16)
        self.TMP = sb("TMP", [128, 512], F32)
        self.Q2B = sb("Q2B", [128, T], BF16)
        self.ATB = sb("ATB", [128, 6, 512], BF16)
        self.XBF = self.ATB
        self.TAB = sb("TAB", [128, 4, T], BF16)
        self.SEL = sb("SEL", [128, 128], BF16)
        self.MSL = sb("MSL", [128, 1024], BF16)
        self.MSD = sb("MSD", [128, 896], BF16)
        self.WPL = sb("WPL", [128, 4, 2048], BF16)
        self.SQS = sb("SQS", [128, 2, 4], BF16)
        self.BCS = sb("BCS", [68, 2, 128], F32)
        self.ONF = sb("ONF", [1, 128], F32)
        self.ONB = sb("ONB", [128, 1], BF16)
        self.ST16 = sb("ST16", [68, T], F32)
        self.B1C = sb("B1C", [128, 2, 2, NKT], F32)
        self.LNB = sb("LNB", [128, 2, 8], F32)
        self.B2C = sb("B2C", [128, 2, 8], F32)
        self.LNG = sb("LNG", [128, 2, 8], F32)
        self.QSC = sb("QSC", [128, 2, 1], F32)
        self.KSC = sb("KSC", [128, 2, 1], F32)
        self.MBLK = sb("MBLK", [128, 17], F32)
        self.MFIN = sb("MFIN", [128, 32], BF16)
        self.MRD = sb("MRD", [128, 2], F32)
        self.OSTG = sb("OSTG", [128, 2, 512], BF16)
        self.NC3 = sb("NC3", [128, 32], F32)
        self.CRW = sb("CRW", [1, 512], F32)
        self.RRW = sb("RRW", [1, 512], F32)
        self.RBC = sb("RBC", [64, 512], F32)
        self.SRW = sb("SRW", [1, 512], F32)
        self.QRW = sb("QRW", [1, 512], F32)
        self.EPS = sb("EPS", [128, 1], F32)
        self.EP2 = sb("EP2", [128, 1], F32)
        self.ZRC = sb("ZRC", [128, 1], F32)

        ov = nc.lookup_mloc(self.OVR).addr
        at = lambda n, s, dt, off: nc.alloc_sbuf_tensor_at(n, s, dt, offset=ov + off)
        self.QG = at("QG", [128, 2, T], BF16, 0)
        self.GLU = at("GLU", [128, 11, 1024], BF16, 0)
        self.KG = at("KG", [128, 2, T], BF16, 8192)
        self.VG = at("VG", [128, 16, 260], BF16, 16384)
        self.OLB = at("OLB", [128, 4, T], BF16, 24704)
        self.VDG = at("VDG", [128, 16, 260], BF16, 0)
        self.ODB = at("ODB", [128, 8, T], BF16, 8448)
        xf = nc.lookup_mloc(self.XF).addr
        self.QD = nc.alloc_sbuf_tensor_at("QD", [128, 8, T], BF16, offset=xf)
        self.KD = nc.alloc_sbuf_tensor_at("KD", [128, 8, T], BF16, offset=xf + 32768)

        ps = lambda n, s: self.ctx.enter_context(nc.psum_tensor(n, s, F32))
        self.SC = ps("SC", [128, 4, 512])
        self.PV0 = ps("PV0", [128, 512])
        self.PV1 = ps("PV1", [128, 512])
        self.GA = ps("GA", [128, 512])
        self.GB = ps("GB", [128, 512])

        self.pl = Plan(nc)
        self.free = {}
        self.prod_t = {}
        self.wslot_free = [None] * 4
        self.ucr = 0
        self.scr = 0
        self.wsl_rr = -1
        self.pend = []
        self.hs_dtoks = [None] * 8

    def produce(self, res, tok):
        self.prod_t[res] = tok

    def prod(self, res):
        return self.prod_t.get(res)

    def consume(self, res, tok):
        self.free[res] = tok

    def fre(self, res):
        return self.free.get(res)

    def dma_g(self, dst, src, waits=(), accum=None):
        if accum is None:
            fn = lambda g, a=dst, b=src: g.dma_start(out=a, in_=b)
        else:
            fn = lambda g, a=dst, b=src, ac=accum: g.dma_start(out=a, in_=b,
                                                               accum_op=ac)
        return self.pl.op("g", fn, waits=waits, inc="W", inck=16)

    def dma_y(self, dst, src, waits=()):
        return self.pl.op("y", lambda y, a=dst, b=src: y.dma_start(out=a, in_=b),
                          waits=waits, inc="X", inck=16)

    def mm(self, out, lhsT, rhs, start, stop, waits=(), inc=None, tp=None):
        return self.pl.op(
            "t", lambda t, o=out, l=lhsT, r=rhs, a=start, b=stop, tp=tp:
            t.matmul(o, l, r, start=a, stop=b, tile_position=tp),
            waits=waits, inc=inc)

    def wslot_pick(self):
        self.wsl_rr = (self.wsl_rr + 1) % 4
        return self.wsl_rr

    def ring6(self, i):
        i = i % 6
        if i < 4:
            return self.SC[:, i, :], ("SC", i), i
        return (self.GA[:, :], "GA", i) if i == 4 else (self.GB[:, :], "GB", i)

    def pump(self, n):
        while len(self.pend) > n:
            self.pend.pop(0)()

    def wp(self, slot, a):
        return self.WPL[:, slot, :].rearrange("p (a c) -> p a c", a=a)

    def preamble(self):
        pl, dy = self.pl, self.dma_y
        dy(self.TAB[...], self.c_tabs[...])
        dy(self.SEL[...], self.c_sel[...])
        dy(self.MSL[...], self.c_ml[...])
        dy(self.MSD[...], self.c_md[...])
        dy(self.SQS[...], self.c_sqsel[...])
        dy(self.BCS[...], self.c_bcsel[...])
        dy(self.ONF[...], self.c_onesf[...])
        dy(self.B1C[...], self.p_b1[...])
        dy(self.LNB[...], self.p_lnb[...])
        dy(self.B2C[...], self.p_b2[...])
        dy(self.LNG[...], self.p_lng[...])
        dy(self.QSC[...], self.p_qs[...])
        dy(self.KSC[...], self.p_ks[...])
        self.tk_const = pl.tok("X")
        pl.op("g", lambda g: g.memset(self.EPS[...], 1e-5))
        pl.op("g", lambda g: g.memset(self.EP2[...], 1e-24))
        pl.op("g", lambda g: g.memset(self.ZRC[...], 0.0))
        pl.op("g", lambda g: g.memset(self.ONB[...], 1.0 / 1024.0), inc="Gm")
        self.tk_ones = pl.tok("Gm")
        self.tk_x = dy(self.XF.ap(),
                       self.xT[0].rearrange("(d p) t -> p d t", p=128))
        pieces = []
        for l in range(2):
            for cc in range(0, 2 * HIDP, 256):
                pieces.append((self.f_w1[l].rearrange("(a p) c -> p a c", p=128)
                               [:, :, cc:cc + 256],
                               self.w1b[l].rearrange("(a p) c -> p a c", p=128)
                               [:, :, cc:cc + 256]))
            for rr in range(0, HIDP, 256):
                pieces.append((self.f_w2[l, rr:rr + 256]
                               .rearrange("(a p) c -> p a c", p=128),
                               self.w2b[l, rr:rr + 256]
                               .rearrange("(a p) c -> p a c", p=128)))
        self._precast_pieces = pieces
        self.tk_precast = None

    def emit_precast(self):
        for srcp, dstp in self._precast_pieces:
            a = srcp.shape[1]
            slot = self.wslot_pick()
            slotap = self.wp(slot, a)
            tk_ld = self.dma_g(slotap, srcp, waits=[self.wslot_free[slot]])
            self.wslot_free[slot] = self.dma_y(dstp, slotap, waits=[tk_ld])
        self._precast_pieces = []
        self.tk_precast = self.pl.tok("X")

    def build(self):
        self.preamble()
        for b in range(NB):
            tk_x = self.tk_x if b == 0 else self.tk_xnext
            self.produce("XF", tk_x)
            self.macaron_ffn(b)
            self.dense_attn(b)
            self.macaron_ffn(b)
            tk_st = self.dma_y(self.outT[b].rearrange("(d p) t -> p d t", p=128),
                               self.XF.ap(), waits=[self.prod("XF")])
            self.consume("XF", tk_st)
            if b + 1 < NB:
                self.tk_xnext = self.dma_y(
                    self.XF.ap(),
                    self.xT[b + 1].rearrange("(d p) t -> p d t", p=128),
                    waits=[tk_st])
        self.pl.op("y", lambda y: y.nop(), waits=[self.pl.tok("X")])
        with self.nc.Block() as block:
            self.pl.emit(block)
        return self.nc

    def macaron_ffn(self, b):
        pl = self.pl
        tk_x = self.prod("XF")
        tk_spill = self.dma_y(self.spill.ap().rearrange("(d p) t -> p d t", p=128),
                              self.XF.ap(), waits=[tk_x, self.fre("spill")])
        last = self.layer_norm(self.XF, "XF", None, tk_x, apply_extra=tk_spill)
        self.produce("XF", last)
        for l in range(2):
            self.inner_layer(b, l)
        tk_h = pl.op("v", lambda v: v.tensor_scalar_mul(
            self.XF.ap().rearrange("p d t -> p (d t)"),
            self.XF.ap().rearrange("p d t -> p (d t)"), 0.5),
            waits=[self.prod("XF")], inc="Vm")
        tk_add = self.dma_g(self.XF.ap(),
                            self.spill.ap().rearrange("(d p) t -> p d t", p=128),
                            waits=[tk_h, tk_spill], accum=OP.add)
        self.consume("spill", tk_add)
        self.produce("XF", tk_add)

    def layer_norm(self, src_f32, dst, affine_l, tk_src, apply_extra=None):
        pl, mm = self.pl, self.mm
        stats_from_hs = src_f32 is None
        last = None
        chain = [None] * 4

        def stats(c):
            cs0 = c * 512
            ma = self.SC[0:1, (c % 2) * 2, :]
            qa = self.SC[0:1, (c % 2) * 2 + 1, :]
            mk_, qk_ = ("SC", (c % 2) * 2), ("SC", (c % 2) * 2 + 1)
            for dt in range(8):
                if not stats_from_hs:
                    tc = pl.op("s", lambda s, dt=dt, cs0=cs0: s.copy(
                        self.XBF[:, dt % 2, :], src_f32[:, dt, cs0:cs0 + 512]),
                        waits=[tk_src, self.fre(("ATB", dt % 2))], inc="Sm")
                    xb = self.XBF[:, dt % 2, :]
                else:
                    tc = tk_src
                    xb = self.HS[:, dt, cs0:cs0 + 512]
                mm(ma, self.ONB[:, :], xb, dt == 0, dt == 7,
                   waits=[tc, self.tk_ones, self.fre(mk_) if dt == 0 else None])
                tq = pl.op("s", lambda s, xb=xb: s.activation(
                    self.Q2B[:, 0:512], xb, AF.Square),
                    waits=[tc, self.fre(("q2", 0))], inc="Sm")
                if not stats_from_hs:
                    self.consume(("ATB", dt % 2), tq)
                tsq = mm(qa, self.ONB[:, :], self.Q2B[:, 0:512],
                         dt == 0, dt == 7,
                         waits=[tq, self.fre(qk_) if dt == 0 else None],
                         inc="Tm")
                self.consume(("q2", 0), tsq)
            sr = self.SRW if c % 2 == 0 else self.RRW
            qr = self.QRW if c % 2 == 0 else self.CRW
            srk = "SRW" if c % 2 == 0 else ("RR", 0)
            qrk = "QRW" if c % 2 == 0 else "CRW"
            tk_mm = pl.tok("Tm")
            pl.op("v", lambda v, sr=sr, ma=ma: v.tensor_copy(sr[...], ma),
                  waits=[tk_mm, self.fre(srk)])
            tk_rows = pl.op("v", lambda v, qr=qr, qa=qa: v.tensor_copy(
                qr[...], qa), waits=[self.fre(qrk)], inc="Vm")
            self.consume(mk_, tk_rows)
            self.consume(qk_, tk_rows)
            chain[c] = (sr, qr, srk, qrk, tk_rows)

        def bcast_apply(c):
            nonlocal last
            cs0 = c * 512
            sr, qr, srk, qrk, tk_rows = chain[c]
            ga, gak = (self.GA, "GA")
            gb, gbk = (self.GB, "GB")
            tga = mm(ga[:, :], self.ONF[...], sr[...], True, True,
                     waits=[tk_rows, self.fre(gak)], inc="Tm")
            pl.op("v", lambda v, sr=sr: v.tensor_mul(sr[...], sr[...], sr[...]),
                  waits=[tga])
            tk_var = pl.op("v", lambda v, sr=sr, qr=qr: v.tensor_sub(
                qr[...], qr[...], sr[...]), inc="Vm")
            self.consume(srk, tk_var)
            pl.op("s", lambda s, qr=qr: s.activation(qr[...], qr[...],
                                                     AF.Ln, bias=self.EPS[0:1, :]),
                  waits=[tk_var])
            tk_rstd = pl.op("s", lambda s, qr=qr: s.activation(
                qr[...], qr[...], AF.Exp, scale=-0.5), inc="Sm")
            tgb = mm(gb[:, :], self.ONF[...], qr[...], True, True,
                     waits=[tk_rstd, self.fre(gbk)], inc="Tm")
            self.consume(qrk, tgb)
            for dt in range(8):
                srcap = (self.HS[:, dt, cs0:cs0 + 512] if stats_from_hs
                         else src_f32[:, dt, cs0:cs0 + 512])
                pl.op("v", lambda v, srcap=srcap, ga=ga: v.tensor_tensor(
                    self.TMP[:, :], srcap, ga[:, :], OP.subtract),
                    waits=[tga, tgb, apply_extra, self.fre("TMP")])
                if dst == "XF":
                    last = pl.op("v", lambda v, dt=dt, cs0=cs0, gb=gb:
                                 v.tensor_tensor(
                                     src_f32[:, dt, cs0:cs0 + 512],
                                     self.TMP[:, :], gb[:, :], OP.mult),
                                 inc="Vm")
                    self.consume("TMP", last)
                    if dt == 7:
                        self.consume(gak, last)
                        self.consume(gbk, last)
                else:
                    tv = pl.op("v", lambda v, gb=gb: v.tensor_tensor(
                        self.TMP[:, :], self.TMP[:, :], gb[:, :],
                        OP.mult), inc="Vm")
                    if dt == 7:
                        self.consume(gak, tv)
                        self.consume(gbk, tv)
                    if affine_l is None:
                        last = pl.op("s", lambda s, dt=dt, cs0=cs0:
                                     s.copy(self.HS[:, dt, cs0:cs0 + 512],
                                            self.TMP[:, :]),
                                     waits=[tv, self.fre("HS")], inc="Sm")
                    else:
                        gc = self.LNG[:, affine_l, dt:dt + 1]
                        bc = self.LNB[:, affine_l, dt:dt + 1]
                        last = pl.op("v", lambda v, dt=dt, cs0=cs0, gc=gc,
                                     bc=bc: v.tensor_scalar(
                                         self.HS[:, dt, cs0:cs0 + 512],
                                         self.TMP[:, :], gc, bc,
                                         OP.mult, OP.add),
                                     waits=[tv, self.fre("HS"), self.tk_const],
                                     inc="Vm")
                    self.consume("TMP", last)

        stats(0)
        ctoks = [None] * 4
        for c in range(1, 4):
            stats(c)
            bcast_apply(c - 1)
            ctoks[c - 1] = last
        bcast_apply(3)
        ctoks[3] = last
        self.ln_ctoks = ctoks
        return last

    def shift_cast(self, tk_src):
        pl = self.pl
        self.hs_dtoks = [None] * 8
        last = None
        for dt in range(8):
            if dt >= 4:
                pl.op("s", lambda s, dt=dt: s.copy(
                    self.HS[:, dt, 1:T], self.XF[:, dt, 0:T - 1]),
                    waits=[tk_src, self.fre("HS")])
                last = pl.op("s", lambda s, dt=dt: s.copy(
                    self.HS[:, dt, 0:1], self.ZRC[...]), inc="Sm")
            else:
                last = pl.op("s", lambda s, dt=dt: s.copy(
                    self.HS[:, dt, :], self.XF[:, dt, :]),
                    waits=[tk_src, self.fre("HS")], inc="Sm")
            self.hs_dtoks[dt] = last
        return last

    def inner_layer(self, b, l):
        pl, mm = self.pl, self.mm
        tk_hs = self.shift_cast(self.prod("XF"))
        self.produce("HS", tk_hs)
        for g in range(2):
            self.local_group(b, l, g, tk_hs)
            if self._precast_pieces:
                self.emit_precast()
        self.consume("HS", pl.tok("Tm"))
        wsl = []
        for s in range(2):
            slot = self.wslot_pick()
            tk = self.dma_g(self.wp(slot, 2),
                            self.f_wo[l, s * 256:(s + 1) * 256]
                            .rearrange("(a p) c -> p a c", p=128),
                            waits=[self.wslot_free[slot]])
            wsl.append((slot, tk))
        for mt in range(8):
            for c in range(4):
                pb, key = ((self.GA, "GA") if (mt * 4 + c) % 2 == 0
                           else (self.GB, "GB"))
                first = True
                tm = None
                for s in range(2):
                    for kk in range(2):
                        lastmm = (s == 1 and kk == 1)
                        tm = mm(pb[:, :], self.wp(wsl[s][0], 2)
                                [:, kk, mt * 128:mt * 128 + 128],
                                self.OLB[:, 2 * s + kk, c * 512:(c + 1) * 512],
                                first, lastmm,
                                waits=[wsl[s][1], self.prod(("O", 2 * s + kk)),
                                       self.fre(key) if first else None],
                                inc="Tm" if lastmm else None)
                        first = False
                tv = pl.op("v", lambda v, mt=mt, c=c, pb=pb: v.tensor_tensor(
                    self.XF[:, mt, c * 512:(c + 1) * 512],
                    self.XF[:, mt, c * 512:(c + 1) * 512], pb[:, :], OP.add),
                    waits=[tm, self.prod("XF")], inc="Vm")
                self.consume(key, tv)
        self.produce("XF", pl.tok("Vm"))
        for s in range(2):
            self.wslot_free[wsl[s][0]] = pl.tok("Tm")
        for pr in range(4):
            self.consume(("O", pr), pl.tok("Tm"))
        self.free["OVR2"] = pl.tok("Tm")
        self.free["OVRg"] = pl.tok("Tm")
        tk_hs2 = self.shift_cast(self.prod("XF"))
        tk_ln = self.layer_norm(None, "HS", l, tk_hs2)
        self.produce("HS", tk_ln)
        self.mlp(b, l, tk_ln)
        self.consume("HS", pl.tok("Tm"))

    def v_gemm(self, wsrc_ap, vbuf, tk_hs, tk_vones):
        pl, mm = self.pl, self.mm
        slot = self.wslot_pick()
        tk_wv = self.dma_g(self.wp(slot, 8), wsrc_ap,
                           waits=[self.wslot_free[slot]])
        vap = vbuf.ap()
        for tt in range(16):
            pb, key = (self.GA, "GA") if tt % 2 == 0 else (self.GB, "GB")
            tm = None
            for dt in range(8):
                tm = mm(pb[:, 0:256], self.HS[:, dt, tt * 128:tt * 128 + 128],
                        self.wp(slot, 8)[:, dt, :], dt == 0, dt == 7,
                        waits=[tk_wv, tk_hs if self.hs_dtoks[dt] is None
                               else self.hs_dtoks[dt],
                               self.fre(key) if dt == 0 else None],
                        inc="Tm" if dt == 7 else None)
            tv = pl.op("v", lambda v, tt=tt, pb=pb, vap=vap: v.tensor_copy(
                bass.AP(vap.tensor, vap.offset + tt * 260,
                        [[vap.ap[0][0], 128], [65, 4], [1, 64]]),
                pb[:, 0:256]),
                waits=[tm, tk_vones, self.fre("VG"), self.fre("OVR")], inc="Vm")
            self.consume(key, tv)
        self.produce("VG", pl.tok("Vm"))
        self.wslot_free[slot] = pl.tok("Vm")

    def local_group(self, b, l, g, tk_hs):
        pl, mm = self.pl, self.mm
        vap = self.VG.ap()
        tk_vones = pl.op("g", lambda gg: gg.memset(
            bass.AP(vap.tensor, vap.offset + 64,
                    [[vap.ap[0][0], 128], [260, 16], [65, 4], [1, 1]]), 1.0),
            waits=[self.fre("VG"), self.fre("OVR")], inc="Gm")
        evk, tk_rk = self.qk_gemm(l, g, "k", tk_hs)
        evq, tk_rq = self.qk_gemm(l, g, "q", tk_hs)
        self.v_gemm(self.f_wqkv[l].rearrange("(a p) c -> p a c", p=128)
                    [:, :, 1024 + g * 256:1024 + g * 256 + 256],
                    self.VG, tk_hs, tk_vones)
        tkk = self.qk_rot(l, g, "k", evk, tk_rk)
        tkq = self.qk_rot(l, g, "q", evq, tk_rq)

        tmr = None
        for pj in range(2):
            for hh in range(2):
                base = 64 * hh
                for blk in range(8):
                    tmr = pl.op("v", lambda v, pj=pj, base=base, blk=blk:
                                v.tensor_reduce(
                                    self.MBLK[base:base + 64,
                                              pj * 8 + blk:pj * 8 + blk + 1],
                                    self.KG[base:base + 64, pj,
                                            blk * 256:(blk + 1) * 256],
                                    AX.X, OP.max, apply_absolute_value=True),
                                waits=[tkk, self.fre("MBLK")], inc="Vm")
        tk_mblk = tmr

        nc3 = {}
        for pj in range(2):
            tk_abs = pl.op("s", lambda s, pj=pj: s.activation(
                self.Q2B[:, :], self.QG[:, pj, :], AF.Abs),
                waits=[tkq] + [self.fre(("q2", c)) for c in range(4)], inc="Sm")
            for c in range(4):
                self.produce(("q2", c), tk_abs)
            nc3[pj] = self.pair_prep(True, g, pj, tk_abs, tk_mblk)
            for c in range(4):
                self.consume(("q2", c), pl.tok("Tm"))
        for pj in range(2):
            for hh in range(2):
                for qc in range(8):
                    self.attn_unit(True, g, pj, hh, qc, hh * 8 + qc,
                                   nc3[pj], 16 * pj)
        self.pump(0)
        for pj in range(2):
            self.consume(("NC3", 16 * pj), pl.tok("Sm"))
        self.consume("MBLK", pl.tok("Vm"))
        self.free[("QK", "q", g, 0)] = pl.tok("Tm")
        self.free[("QK", "q", g, 1)] = pl.tok("Tm")
        self.free[("QK", "k", g, 0)] = pl.tok("Tm")
        self.free[("QK", "k", g, 1)] = pl.tok("Tm")
        self.consume("VG", pl.tok("Tm"))

    def qk_gemm(self, l, g, side, tk_hs):
        pl, mm = self.pl, self.mm
        dstbuf = self.KG if side == "k" else self.QG
        col0 = (512 if side == "k" else 0) + g * 256
        bs = 0 if side == "k" else 64
        slot = self.wslot_pick()
        tk_w = self.dma_g(self.wp(slot, 8),
                          self.f_wqkv[l].rearrange("(a p) c -> p a c", p=128)
                          [:, :, col0:col0 + 256],
                          waits=[self.wslot_free[slot]])
        ev = {}
        for pj in range(2):
            for c in range(4):
                cs0 = c * 512
                pb, key = (self.GA, "GA") if c % 2 == 0 else (self.GB, "GB")
                tm = None
                for dt in range(8):
                    tm = mm(pb[:, :],
                            self.wp(slot, 8)[:, dt, pj * 128:pj * 128 + 128],
                            self.HS[:, dt, cs0:cs0 + 512], dt == 0, dt == 7,
                            waits=[tk_w, self.hs_dtoks[dt],
                                   self.fre(key) if dt == 0 else None],
                            inc="Tm" if dt == 7 else None)
                tq = pl.op("s", lambda s, cs0=cs0, pb=pb: s.activation(
                    self.Q2B[:, cs0:cs0 + 512], pb[:, :], AF.Square),
                    waits=[tm, self.fre(("q2", c))], inc="Sm")
                te = pl.op("v", lambda v, pj=pj, cs0=cs0, pb=pb, dstbuf=dstbuf:
                           v.tensor_copy(dstbuf[:, pj, cs0:cs0 + 512], pb[:, :]),
                           waits=[tm, tq, self.fre(("QK", side, g, pj)),
                                  self.fre("OVR")], inc="Vm")
                ev[(pj, c)] = te
                self.consume(key, te)
                tsq = mm(self.SC[0:4, c, :], self.SQS[:, pj, :],
                         self.Q2B[:, cs0:cs0 + 512], pj == 0, pj == 1,
                         waits=[tq, self.tk_const,
                                self.fre(("SC", c)) if pj == 0 else None],
                         inc="Tm")
                self.consume(("q2", c), tsq)
        tk_sq = pl.tok("Tm")
        tcp = pl.op("v", lambda v, bs=bs: v.tensor_copy(
            self.ST16[bs:bs + 4, :].rearrange("h (c t) -> h c t", c=4),
            self.SC[0:4, :, :]), waits=[tk_sq, self.fre(("ST", bs))],
            inc="Vm")
        for c in range(4):
            self.consume(("SC", c), tcp)
        pl.op("s", lambda s, bs=bs: s.activation(
            self.ST16[bs:bs + 4, :], self.ST16[bs:bs + 4, :], AF.Ln,
            bias=self.EP2[0:4, :]), waits=[tcp])
        tk_rstd = pl.op("s", lambda s, bs=bs: s.activation(
            self.ST16[bs:bs + 4, :], self.ST16[bs:bs + 4, :], AF.Exp,
            scale=-0.5), inc="Sm")
        self.wslot_free[slot] = pl.tok("Tm")
        return ev, tk_rstd

    def qk_rot(self, l, g, side, ev, tk_rstd):
        pl, mm = self.pl, self.mm
        dstbuf = self.KG if side == "k" else self.QG
        scol = self.KSC if side == "k" else self.QSC
        tabc = self.TAB[:, 2 if side == "k" else 0, :]
        tabs_ = self.TAB[:, 3 if side == "k" else 1, :]
        bs = 0 if side == "k" else 64
        last = None
        for pj in range(2):
            for c in range(4):
                cs0 = c * 512
                tba = mm(self.GA[:, :], self.BCS[bs:bs + 4, pj, :],
                         self.ST16[bs:bs + 4, cs0:cs0 + 512],
                         True, True,
                         waits=[tk_rstd, self.tk_const, self.fre("GA")],
                         inc="Tm")
                tsc = pl.op("v", lambda v, pj=pj, cs0=cs0, dstbuf=dstbuf,
                            scol=scol: v.scalar_tensor_tensor(
                                dstbuf[:, pj, cs0:cs0 + 512],
                                dstbuf[:, pj, cs0:cs0 + 512], scol[:, l, :],
                                self.GA[:, :], OP.mult, OP.mult),
                            waits=[tba, ev[(pj, c)], self.tk_const], inc="Vm")
                self.consume("GA", tsc)
                ta = pl.op("v", lambda v, pj=pj, cs0=cs0, dstbuf=dstbuf,
                           tabs_=tabs_: v.tensor_tensor(
                               self.Q2B[:, cs0:cs0 + 512],
                               dstbuf[:, pj, cs0:cs0 + 512],
                               tabs_[:, cs0:cs0 + 512], OP.mult),
                           waits=[self.fre(("q2", c))], inc="Vm")
                trm = mm(self.GB[:, :], self.SEL[...], self.Q2B[:, cs0:cs0 + 512],
                         True, True, waits=[ta, self.fre("GB")], inc="Tm")
                self.consume(("q2", c), trm)
                tb = pl.op("v", lambda v, pj=pj, cs0=cs0, dstbuf=dstbuf,
                           tabc=tabc: v.tensor_tensor(
                               self.TMP[:, :], dstbuf[:, pj, cs0:cs0 + 512],
                               tabc[:, cs0:cs0 + 512], OP.mult),
                           waits=[self.fre("TMP")])
                last = pl.op("v", lambda v, pj=pj, cs0=cs0, dstbuf=dstbuf:
                             v.tensor_tensor(dstbuf[:, pj, cs0:cs0 + 512],
                                             self.TMP[:, :], self.GB[:, :],
                                             OP.add), waits=[trm], inc="Vm")
                self.consume("GB", last)
                self.consume("TMP", last)
        self.consume(("ST", bs), pl.tok("Tm"))
        return last

    def pair_prep(self, local, g, pj, tk_abs, tk_mblk, qsrc=None):
        gen = self.pair_prep_gen(local, g, pj, tk_abs, tk_mblk, qsrc)
        for _ in gen:
            pass
        return self.last_nc3

    def pair_prep_gen(self, local, g, pj, tk_abs, tk_mblk, qsrc=None):
        if qsrc is None:
            qsrc = self.Q2B
        """Batched negc3 bias columns for one pair's units into
        NC3[:, off + u]. off = 16*(pj%2). Returns token of the NC3 copies."""
        pl, mm = self.pl, self.mm
        nqc = 8 if local else 4
        qlen = 256 if local else 512
        off = 16 * (pj % 2)
        tred = {}
        tnc = None
        nu = 2 * nqc
        for hh in range(2):
            base = 64 * hh
            for qc in range(nqc):
                u = hh * nqc + qc
                q0 = qc * qlen
                if local:
                    blk0 = max(0, qc - 2)
                    tmf = pl.op("v", lambda v, base=base, pj=pj, blk0=blk0,
                                qc=qc, u=u: v.tensor_reduce(
                                    self.MRD[base:base + 64, u % 2:u % 2 + 1],
                                    self.MBLK[base:base + 64,
                                              pj * 8 + blk0:pj * 8 + qc + 1],
                                    AX.X, OP.max),
                                waits=[tk_mblk, self.fre(("MRD", u % 2))],
                                inc="Vm")
                    msrc = self.MRD[base:base + 64, u % 2:u % 2 + 1]
                else:
                    tmf = tk_mblk
                    msrc = self.MBLK[base:base + 64, pj:pj + 1]
                tmc = pl.op("s", lambda s, msrc=msrc, base=base, u=u, off=off:
                            s.activation(self.MFIN[base:base + 64,
                                                   off + u:off + u + 1],
                                         msrc, AF.Copy, scale=-1.0),
                            waits=[tmf, self.fre(("MFIN", off))], inc="Sm")
                if local:
                    self.consume(("MRD", u % 2), tmc)
                slot = u % 4
                tcm = mm(self.SC[0:1, slot, 0:qlen],
                         self.MFIN[base:base + 64, off + u:off + u + 1],
                         qsrc[base:base + 64, q0:q0 + qlen], True, True,
                         waits=[tmc, tk_abs, self.fre(("SC", slot))],
                         inc="Tm")
                trd = pl.op("v", lambda v, u=u, qlen=qlen, off=off, slot=slot:
                            v.tensor_reduce(
                                self.CRW[0:1, off + u:off + u + 1],
                                self.SC[0:1, slot, 0:qlen], AX.X, OP.min),
                            waits=[tcm,
                                   self.fre("CRW") if u == 0 else None],
                            inc="Vm")
                tred[u] = trd
                tbc = mm(self.SC[:, slot, 511:512], self.ONF[...],
                         self.CRW[0:1, off + u:off + u + 1], True, True,
                         waits=[trd], inc="Tm")
                self.consume(("SC", slot), tbc)
                if u % 4 == 3 or u == nu - 1:
                    r0 = (u // 4) * 4
                    nn = u - r0 + 1
                    tnc = pl.op("v", lambda v, r0=r0, nn=nn, off=off:
                                v.tensor_copy(
                                    self.NC3[:, off + r0:off + r0 + nn],
                                    self.SC[:, r0 % 4:r0 % 4 + nn, 511]
                                    if False else
                                    bass.AP(self.SC.ap().tensor,
                                            self.SC.ap().offset + (0) * 0 + 511,
                                            [[self.SC.ap().ap[0][0], 128],
                                             [512, nn]]),
                                    ),
                                waits=[tbc, self.fre(("NC3", off))
                                       if r0 == 0 else None], inc="Vm")
                    for uu in range(r0, u + 1):
                        self.consume(("SC", uu % 4), tnc)
                yield None
        self.consume("CRW", pl.tok("Tm"))
        self.consume(("MFIN", off), pl.tok("Tm"))
        self.last_nc3 = tnc

    def attn_unit(self, local, g, pj, hh, qc, u, tk_nc3, off):
        pl, mm = self.pl, self.mm
        base = 64 * hh
        hg = 2 * pj + hh
        if local:
            qbuf, kbuf, vbuf, obuf = self.QG, self.KG, self.VG, self.OLB
            pslot, oslot = pj, 2 * g + pj
            q0, qlen = qc * 256, 256
            ms = [m_ for m_ in (-512, -384, -256, -128, 0, 128)
                  if qc * 256 + m_ >= 0]
            kts = [(qc * 256 + m_) // 128 for m_ in ms]
        else:
            qbuf, kbuf, vbuf, obuf = self.QD, self.KD, self.VDG, self.ODB
            pslot = oslot = 2 * g + pj
            q0, qlen = qc * 512, 512
            ms = None
            kts = list(range(4 * (qc + 1)))
        nkt = len(kts)
        ucr = self.ucr
        self.ucr += 1
        pvb = self.PV0 if ucr % 2 == 0 else self.PV1
        pvi = ucr % 2
        per = 512 // qlen
        vap = vbuf.ap()
        uctx = {"j": 0, "tk_pv": None}

        allsi = list(range(0, nkt, per))
        for blk0 in range(0, len(allsi), 3):
            self.pump(1)
            blksi = allsi[blk0:blk0 + 3]
            subs = []
            for si in blksi:
                sub = kts[si:si + per]
                ap6, key6, idx = self.ring6(self.scr)
                self.scr += 1
                tm = None
                for i, kt in enumerate(sub):
                    tm = mm(ap6[:, i * qlen:(i + 1) * qlen],
                            kbuf[base:base + 64, pslot, kt * 128:kt * 128 + 128],
                            qbuf[base:base + 64, pslot, q0:q0 + qlen],
                            True, True,
                            waits=[self.fre(key6) if i == 0 else None],
                            inc="Tm" if i == len(sub) - 1 else None)
                texp = pl.op("s", lambda s, ap6=ap6, idx=idx, u=u, off=off:
                             s.activation(self.ATB[:, idx, :], ap6, AF.Exp,
                                          bias=self.NC3[:, off + u:off + u + 1]),
                             waits=[tm, tk_nc3, self.fre(("ATB", idx))],
                             inc="Sm")
                self.consume(key6, texp)
                tmk = texp
                if local:
                    msl = None
                    if ms[si] == -512:
                        msl = self.MSL[:, 0:512]
                    elif ms[si] == 0:
                        msl = self.MSL[:, 512:1024]
                    if msl is not None:
                        tmk = pl.op("v", lambda v, idx=idx, msl=msl:
                                    v.tensor_tensor(self.ATB[:, idx, :],
                                                    self.ATB[:, idx, :], msl,
                                                    OP.mult), waits=[texp],
                                    inc="Vm")
                else:
                    sd = sub[0] - 4 * qc
                    if sd >= 0:
                        o2 = 384 - 128 * sd
                        tmk = pl.op("v", lambda v, idx=idx, o2=o2:
                                    v.tensor_tensor(self.ATB[:, idx, :],
                                                    self.ATB[:, idx, :],
                                                    self.MSD[:, o2:o2 + 512],
                                                    OP.mult), waits=[texp],
                                    inc="Vm")
                subs.append((sub, idx, tmk))

            def pvblk(subs=subs, uctx=uctx, pvb=pvb, pvi=pvi, qlen=qlen,
                      nkt=nkt, hg=hg, vap=vap):
                for sub, idx, tmk in subs:
                    tpv = None
                    for i, kt in enumerate(sub):
                        j = uctx["j"]
                        uctx["j"] += 1
                        lhsT = bass.AP(vap.tensor,
                                       vap.offset + kt * 260 + hg * 65,
                                       [[vap.ap[0][0], 128], [1, 65]])
                        tpv = mm(pvb[0:65, 0:qlen], lhsT,
                                 self.ATB[:, idx, i * qlen:(i + 1) * qlen],
                                 j == 0, j == nkt - 1,
                                 waits=[tmk, self.prod("VG"),
                                        self.fre(("PV", pvi)) if j == 0 else None],
                                 inc="Tm")
                    self.consume(("ATB", idx), tpv)
                uctx["tk_pv"] = pl.tok("Tm")
            self.pend.append(pvblk)

        def norm(uctx=uctx, pvb=pvb, pvi=pvi, qlen=qlen, base=base,
                 oslot=oslot, q0=q0, obuf=obuf, ucr=ucr):
            tk_pv = uctx["tk_pv"]
            rr = self.RRW if ucr % 2 == 0 else self.SRW
            rrk = ("RR", 0) if ucr % 2 == 0 else "SRW"
            tln = pl.op("s", lambda s, pvb=pvb, qlen=qlen, rr=rr: s.activation(
                rr[0:1, 0:qlen], pvb[64:65, 0:qlen], AF.Ln),
                waits=[tk_pv, self.fre(rrk)], inc="Sm")
            tcp = pl.op("v", lambda v, pvb=pvb, qlen=qlen, ucr=ucr:
                        v.tensor_copy(self.OSTG[0:64, ucr % 2, 0:qlen],
                                      pvb[0:64, 0:qlen]),
                        waits=[tk_pv, self.fre(("OST", ucr % 2))], inc="Vm")
            tex = pl.op("s", lambda s, qlen=qlen, rr=rr: s.activation(
                rr[0:1, 0:qlen], rr[0:1, 0:qlen], AF.Exp, scale=-1.0),
                inc="Sm")
            tb2 = mm(pvb[64:128, 0:qlen], self.ONF[0:1, 0:64],
                     rr[0:1, 0:qlen], True, True,
                     waits=[tex, tcp], inc="Tm", tp=(0, 64))
            self.free[rrk] = tb2
            tst = pl.op("v", lambda v, base=base, oslot=oslot, q0=q0,
                        qlen=qlen, obuf=obuf, ucr=ucr, pvb=pvb:
                        v.scalar_tensor_tensor(
                            obuf[base:base + 64, oslot, q0:q0 + qlen],
                            self.OSTG[0:64, ucr % 2, 0:qlen], 1.0,
                            pvb[64:128, 0:qlen], OP.mult, OP.mult),
                        waits=[tb2, self.fre("OVR2")], inc="Vm")
            self.consume(("PV", pvi), tst)
            self.consume(("OST", ucr % 2), tst)
            self.produce(("O", oslot), tst)
        self.pend.append(norm)

    def mlp(self, b, l, tk_ln):
        pl, mm = self.pl, self.mm
        ps8 = [lambda s=s: self.SC[:, s, :] for s in range(4)] + \
              [lambda: self.PV0[:, :], lambda: self.PV1[:, :],
               lambda: self.GA[:, :], lambda: self.GB[:, :]]
        ps8k = [("SC", 0), ("SC", 1), ("SC", 2), ("SC", 3),
                ("PV", 0), ("PV", 1), "GA", "GB"]
        for cb in range(2):
            tb0 = cb * 1024
            for ktbi, nkt_b in enumerate(KTB):
                kt0 = 11 * ktbi
                for hf in range(2):
                    hoff = hf * HIDP
                    for cc in range(kt0 * 128, (kt0 + nkt_b) * 128, 256):
                        ckts = min(2, kt0 + nkt_b - cc // 128)
                        slot = self.wslot_pick()
                        tkw = self.dma_g(
                            self.wp(slot, 8)[:, :, 0:128 * ckts],
                            self.w1b[l].rearrange("(a p) c -> p a c", p=128)
                            [:, :, hoff + cc:hoff + cc + 128 * ckts],
                            waits=[self.wslot_free[slot], self.tk_precast])
                        for ii in range(ckts):
                            kt = cc // 128 + ii
                            ks = kt - kt0
                            for n in range(2):
                                ns0 = tb0 + n * 512
                                pb, key = [[(self.GA, "GA"), (self.PV0, ("PV", 0))],
                                           [(self.GB, "GB"), (self.PV1, ("PV", 1))]][hf][n]
                                tkc = self.ln_ctoks[ns0 // 512]
                                tm = None
                                for dt in range(8):
                                    tm = mm(pb[:, :],
                                            self.wp(slot, 8)[:, dt,
                                                             ii * 128:ii * 128 + 128],
                                            self.HS[:, dt, ns0:ns0 + 512],
                                            dt == 0, dt == 7,
                                            waits=[tkw, tkc,
                                                   self.fre(key) if dt == 0 else None],
                                            inc="Tm" if dt == 7 else None)
                                if hf == 0:
                                    te = pl.op(
                                        "v", lambda v, pb=pb, kt=kt, ks=ks, n=n:
                                        v.tensor_scalar_add(
                                            self.GLU[:, ks, n * 512:n * 512 + 512],
                                            pb[:, :],
                                            self.B1C[:, l, 0, kt:kt + 1]),
                                        waits=[tm, self.tk_const,
                                               self.fre(("GLU", ks, n)),
                                               self.fre("OVRg")],
                                        inc="Vm")
                                    self.consume(key, te)
                                    self.free[("GLUa", ks, n)] = te
                                else:
                                    tg = pl.op(
                                        "s", lambda s, pb=pb, kt=kt:
                                        s.activation(self.XBF[:, 0, :], pb[:, :],
                                                     AF.Gelu,
                                                     bias=self.B1C[:, l, 1, kt:kt + 1]),
                                        waits=[tm, self.tk_const,
                                               self.fre(("ATB", 0))], inc="Sm")
                                    self.consume(key, tg)
                                    tglu = pl.op(
                                        "v", lambda v, ks=ks, n=n:
                                        v.tensor_tensor(
                                            self.GLU[:, ks, n * 512:n * 512 + 512],
                                            self.GLU[:, ks, n * 512:n * 512 + 512],
                                            self.XBF[:, 0, :], OP.mult),
                                        waits=[tg, self.free.get(("GLUa", ks, n))],
                                        inc="Vm")
                                    self.consume(("ATB", 0), tglu)
                                    self.produce(("GLUr", ks, n), tglu)
                        self.wslot_free[slot] = pl.tok("Tm")
                for n in range(2):
                    ns0 = tb0 + n * 512
                    slot = None
                    for ki, kt in enumerate(range(kt0, kt0 + nkt_b)):
                        ks = kt - kt0
                        if ki % 2 == 0:
                            if slot is not None:
                                self.wslot_free[slot] = pl.tok("Tm")
                            nr = min(2, nkt_b - ki)
                            slot = self.wslot_pick()
                            tkw2 = self.dma_g(
                                self.wp(slot, 2)[:, 0:nr, :],
                                self.w2b[l, kt * 128:kt * 128 + 128 * nr]
                                .rearrange("(a p) c -> p a c", p=128),
                                waits=[self.wslot_free[slot], self.tk_precast])
                        tm = None
                        for mt in range(8):
                            tm = mm(ps8[mt](),
                                    self.wp(slot, 2)[:, ki % 2,
                                                     mt * 128:mt * 128 + 128],
                                    self.GLU[:, ks, n * 512:n * 512 + 512],
                                    ki == 0, ki == nkt_b - 1,
                                    waits=[tkw2, self.prod(("GLUr", ks, n)),
                                           self.fre(ps8k[mt]) if ki == 0 else None],
                                    inc="Tm" if mt == 7 else None)
                    self.wslot_free[slot] = pl.tok("Tm")
                    tmm = pl.tok("Tm")
                    for mt in range(8):
                        if ktbi == 0:
                            tv = pl.op("v", lambda v, mt=mt, ns0=ns0:
                                       v.tensor_tensor(
                                           self.XF[:, mt, ns0:ns0 + 512],
                                           self.XF[:, mt, ns0:ns0 + 512],
                                           ps8[mt](), OP.add),
                                       waits=[tmm, self.prod("XF")], inc="Vm")
                        else:
                            tv = pl.op("v", lambda v, mt=mt, ns0=ns0:
                                       v.scalar_tensor_tensor(
                                           self.XF[:, mt, ns0:ns0 + 512],
                                           ps8[mt](), self.B2C[:, l, mt:mt + 1],
                                           self.XF[:, mt, ns0:ns0 + 512],
                                           OP.add, OP.add),
                                       waits=[tmm, self.prod("XF"),
                                              self.tk_const], inc="Vm")
                        self.consume(ps8k[mt], tv)
                    for kk in range(nkt_b):
                        self.consume(("GLU", kk, n), pl.tok("Tm"))
                self.produce("XF", pl.tok("Vm"))
        self.produce("XF", pl.tok("Vm"))
        self.free["OVR"] = pl.tok("Tm")

    def dense_attn(self, b):
        pl, mm = self.pl, self.mm
        tk_ln = self.layer_norm(self.XF, "HS", None, self.prod("XF"))
        self.produce("HS", tk_ln)
        tk_spill = self.dma_y(self.spill.ap().rearrange("(d p) t -> p d t", p=128),
                              self.XF.ap(),
                              waits=[self.prod("XF"), self.fre("spill")])
        self.free["OVR"] = pl.tok("Tm")
        self.free["OVR2"] = pl.tok("Tm")
        for side in range(2):
            dst = self.QD if side == 0 else self.KD
            scl = 0.125 if side == 0 else 1.0
            for ch in range(4):
                col0 = side * 1024 + ch * 256
                slot = self.wslot_pick()
                tkw = self.dma_g(self.wp(slot, 8),
                                 self.wqkv_d.rearrange("(a p) c -> p a c", p=128)
                                 [:, :, col0:col0 + 256],
                                 waits=[self.wslot_free[slot]])
                for sub in range(2):
                    p = ch * 2 + sub
                    for c in range(4):
                        cs0 = c * 512
                        pb, key = ((self.GA, "GA") if (p * 4 + c) % 2 == 0
                                   else (self.GB, "GB"))
                        tm = None
                        for dt in range(8):
                            tm = mm(pb[:, :],
                                    self.wp(slot, 8)[:, dt, sub * 128:sub * 128 + 128],
                                    self.HS[:, dt, cs0:cs0 + 512], dt == 0,
                                    dt == 7,
                                    waits=[tkw, tk_ln, tk_spill,
                                           self.fre(key) if dt == 0 else None],
                                    inc="Tm" if dt == 7 else None)
                        te = pl.op("s", lambda s, p=p, cs0=cs0, pb=pb, dst=dst,
                                   scl=scl: s.activation(
                                       dst[:, p, cs0:cs0 + 512], pb[:, :],
                                       AF.Copy, scale=scl),
                                   waits=[tm], inc="Sm")
                        self.consume(key, te)
                self.wslot_free[slot] = pl.tok("Sm")
        tk_qk = pl.tok("Sm")
        self.produce("QD", tk_qk)
        self.produce("KD", tk_qk)

        for g in range(4):
            vap = self.VDG.ap()
            tk_vones = pl.op("g", lambda gg: gg.memset(
                bass.AP(vap.tensor, vap.offset + 64,
                        [[vap.ap[0][0], 128], [260, 16], [65, 4], [1, 1]]), 1.0),
                waits=[self.fre("VG"), self.fre("OVR")], inc="Gm")
            self.v_gemm(self.wqkv_d.rearrange("(a p) c -> p a c", p=128)
                        [:, :, 2048 + g * 256:2048 + g * 256 + 256],
                        self.VDG, tk_ln, tk_vones)

            nc3 = {}
            for pj in range(2):
                p = 2 * g + pj
                tmr = None
                for hh in range(2):
                    base = 64 * hh
                    tmr = pl.op("v", lambda v, base=base, p=p, pj=pj:
                                v.tensor_reduce(
                                    self.MBLK[base:base + 64, pj:pj + 1],
                                    self.KD[base:base + 64, p, :],
                                    AX.X, OP.max,
                                    apply_absolute_value=True),
                                waits=[tk_qk, self.fre("MBd")],
                                inc="Vm")
                tk_abs = pl.op("s", lambda s, p=p: s.activation(
                    self.Q2B[:, :], self.QD[:, p, :], AF.Abs),
                    waits=[(tk_qk if pj == 0 else pl.tok("Tm"))] +
                          [self.fre(("q2", c)) for c in range(4)],
                    inc="Sm")
                for c in range(4):
                    self.produce(("q2", c), tk_abs)
                nc3[pj] = self.pair_prep(False, g, pj, tk_abs, tmr)
                self.free["MBd"] = pl.tok("Sm")
                for c in range(4):
                    self.consume(("q2", c), pl.tok("Tm"))
            for pj in range(2):
                for hh in range(2):
                    for qc in range(4):
                        self.attn_unit(False, g, pj, hh, qc, hh * 4 + qc,
                                       nc3[pj], 16 * pj)
            self.pump(0)
            for pj in range(2):
                self.consume(("NC3", 16 * pj), pl.tok("Sm"))

        tk_rel = self.dma_y(self.XF.ap(),
                            self.spill.ap().rearrange("(d p) t -> p d t", p=128),
                            waits=[tk_spill, pl.tok("Tm"), pl.tok("Sm")])
        self.consume("spill", tk_rel)
        for ch in range(4):
            slot = self.wslot_pick()
            tkw = self.dma_g(self.wp(slot, 8),
                             self.wout_d.rearrange("(a p) c -> p a c", p=128)
                             [:, :, ch * 256:ch * 256 + 256],
                             waits=[self.wslot_free[slot]])
            for sub in range(2):
                mt = ch * 2 + sub
                for c in range(4):
                    cs0 = c * 512
                    pb, key = ((self.GA, "GA") if (mt * 4 + c) % 2 == 0
                               else (self.GB, "GB"))
                    tm = None
                    for p in range(8):
                        tm = mm(pb[:, :],
                                self.wp(slot, 8)[:, p, sub * 128:sub * 128 + 128],
                                self.ODB[:, p, cs0:cs0 + 512], p == 0, p == 7,
                                waits=[tkw, self.prod(("O", p)),
                                       self.fre(key) if p == 0 else None],
                                inc="Tm" if p == 7 else None)
                    tv = pl.op("v", lambda v, mt=mt, cs0=cs0, pb=pb:
                               v.tensor_tensor(
                                   self.XF[:, mt, cs0:cs0 + 512],
                                   self.XF[:, mt, cs0:cs0 + 512], pb[:, :],
                                   OP.add), waits=[tm, tk_rel], inc="Vm")
                    self.consume(key, tv)
            self.wslot_free[slot] = pl.tok("Vm")
        self.consume("VG", pl.tok("Tm"))
        self.consume("HS", pl.tok("Tm"))
        self.free["OVR"] = pl.tok("Tm")
        self.free["OVR2"] = pl.tok("Tm")
        self.free["OVRg"] = pl.tok("Tm")
        for pr in range(4):
            self.free[("QK", "q", pr // 2, pr % 2)] = pl.tok("Tm")
            self.free[("QK", "k", pr // 2, pr % 2)] = pl.tok("Tm")
        self.produce("XF", pl.tok("Vm"))


def kernel(**inputs):
    maps = _host_prep(inputs)
    bld = Builder()
    nc = bld.build()
    res = run_bass_kernel_spmd(nc, maps, core_ids=list(range(8)))
    out = np.stack([r["outT"] for r in res.results])
    out = out.transpose(0, 1, 3, 2).reshape(16, T, D)
    return np.ascontiguousarray(out.astype(np.float32))
